# version 25
# baseline (speedup 1.0000x reference)
import numpy as np
import contextlib
import concourse.bass as bass
import concourse.mybir as mybir
from concourse.bass_utils import run_bass_kernel_spmd

F32 = mybir.dt.float32
BF16 = mybir.dt.bfloat16
AF = mybir.ActivationFunctionType
ALU = mybir.AluOpType
AX = mybir.AxisListType

B = 8
T = 2048
D = 2048
H = 16
R = 16
RF = 8
NTT = T // 128
NDC = D // 128
NTC = 4
NOC = 4
RT = float(128 ** -0.25)
S4 = 0.5
FS = float(1.0 / np.sqrt(RF))
EPS = 1e-5

ENGS = ("sync", "gpsimd", "vector", "scalar", "tensor")
SEMS = ("sW", "sX", "sST", "sPE", "sDVE", "sACT", "sG")


class Sched:

    def __init__(self):
        self.ops = {e: [] for e in ENGS}
        self.cnt = {s: 0 for s in SEMS}
        self.lastwait = {e: {s: 0 for s in SEMS} for e in ENGS}

    def emit(self, eng, builder, waits=(), inc=None):
        ws = []
        for (s, v) in waits:
            if v > self.lastwait[eng][s]:
                ws.append((s, v))
                self.lastwait[eng][s] = v
        post = None
        if inc is not None:
            s, amt = inc
            self.cnt[s] += amt
            post = self.cnt[s]
        self.ops[eng].append((ws, builder, inc))
        return post

    def replay(self, eng_name, eng, sems):
        for ws, builder, inc in self.ops[eng_name]:
            for (s, v) in ws:
                eng.wait_ge(sems[s], v)
            instr = builder(eng)
            if inc is not None:
                s, amt = inc
                instr.then_inc(sems[s], amt)


def build_program():
    nc = bass.Bass("TRN2", target_bir_lowering=False, debug=False)

    xd = nc.dram_tensor("x", [T, D], F32, kind="ExternalInput").ap()
    wqd = nc.dram_tensor("wq", [H, 128, NDC, 128], F32, kind="ExternalInput").ap()
    wkd = nc.dram_tensor("wk", [H, 128, NDC, 128], F32, kind="ExternalInput").ap()
    wvd = nc.dram_tensor("wv", [H // 2, 128, NDC, 256], F32, kind="ExternalInput").ap()
    wod = nc.dram_tensor("wo", [8, 128, NDC, 256], F32, kind="ExternalInput").ap()
    bqcd = nc.dram_tensor("bqc", [128, H], F32, kind="ExternalInput").ap()
    bkcd = nc.dram_tensor("bkc", [128, H], F32, kind="ExternalInput").ap()
    bvrd = nc.dram_tensor("bvr", [1, D], F32, kind="ExternalInput").ap()
    browd = nc.dram_tensor("brow", [2, D], F32, kind="ExternalInput").ap()
    btd = nc.dram_tensor("bt", [3, 128, NDC * R], F32, kind="ExternalInput").ap()
    botd = nc.dram_tensor("bot", [128, NDC * R], F32, kind="ExternalInput").ap()
    atd = nc.dram_tensor("at", [3, R, D], F32, kind="ExternalInput").ap()
    aotd = nc.dram_tensor("aot", [R, D], F32, kind="ExternalInput").ap()
    factd = nc.dram_tensor("fact", [RF, 8, 4, 128, 1024], F32, kind="ExternalInput").ap()
    eyed = nc.dram_tensor("eye", [128, 128], F32, kind="ExternalInput").ap()
    outd = nc.dram_tensor("out", [T, D], F32, kind="ExternalOutput").ap()

    sch = Sched()
    C = sch.cnt

    ctx = contextlib.ExitStack()
    L = lambda name, shape, dt: ctx.enter_context(nc.sbuf_tensor(name, shape, dt, side="left"))

    xn_T = L("xn_T", [128, NDC * T], BF16)
    ones_b = L("ones_b", [128, 128], BF16)
    bqc = L("bqc_s", [128, H], F32)
    bkc = L("bkc_s", [128, H], F32)
    rowpack = L("rowpack", [128, D], BF16)
    sel0 = L("sel0", [128, 128], BF16)
    sel32 = L("sel32", [128, 128], BF16)
    bot_b = L("bot_b", [128, NDC * R], BF16)
    aq_f = L("aq_f", [128, D], BF16)
    ak_f = L("ak_f", [128, D], BF16)
    av_f = L("av_f", [128, D], BF16)
    zpack = L("zpack", [128, T], BF16)
    stats = L("stats", [128, 8], F32)
    negc = L("negc", [128, H], F32)
    mxcol = L("mxcol", [128, 8], F32)

    sems = {s: ctx.enter_context(nc.semaphore(s)) for s in SEMS}

    for fill in (lambda v: v.memset(ones_b[:, :], 1.0),
                 lambda v: v.memset(rowpack[:, :], 0.0),
                 lambda v: v.memset(sel0[:, :], 0.0),
                 lambda v: v.memset(sel32[:, :], 0.0),
                 lambda v: v.memset(aq_f[:, :], 0.0),
                 lambda v: v.memset(ak_f[:, :], 0.0),
                 lambda v: v.memset(av_f[:, :], 0.0),
                 lambda v: v.memset(zpack[:, :], 0.0)):
        sch.emit("vector", fill, inc=("sDVE", 1))
    mset0 = C["sDVE"]
    sch.emit("vector", lambda v: v.memset(sel0[0:1, :], 1.0), waits=[("sDVE", mset0)], inc=("sDVE", 1))
    sch.emit("vector", lambda v: v.memset(sel32[32:33, :], 1.0), inc=("sDVE", 1))
    mset = C["sDVE"]
    sch.emit("gpsimd", lambda g: g.dma_start(out=eye_b[:, :], in_=eyed[:, :]), inc=("sW", 16))
    sch.emit("gpsimd", lambda g: g.dma_start(out=rowpack[0:1, :], in_=bvrd[:, :]),
             waits=[("sDVE", mset)], inc=("sW", 16))
    for i in range(3):
        sch.emit("gpsimd", lambda g, i=i: g.dma_start(out=bt_b[:, i * NDC * R:(i + 1) * NDC * R], in_=btd[i]), inc=("sW", 16))
    sch.emit("gpsimd", lambda g: g.dma_start(out=bot_b[:, :], in_=botd[:, :]), inc=("sW", 16))
    for i, af in enumerate(("aq_f", "ak_f", "av_f")):
        dst = {"aq_f": None}
    sch.emit("gpsimd", lambda g: g.dma_start(out=aq_f[0:R, :], in_=atd[0]), waits=[("sDVE", mset)], inc=("sW", 16))
    sch.emit("gpsimd", lambda g: g.dma_start(out=ak_f[32:32 + R, :], in_=atd[1]), inc=("sW", 16))
    sch.emit("gpsimd", lambda g: g.dma_start(out=av_f[64:64 + R, :], in_=atd[2]), inc=("sW", 16))
    w_const = C["sW"]
    sch.emit("sync", lambda s: s.dma_start(out=bqc[:, :], in_=bqcd[:, :]), inc=("sX", 16))
    sch.emit("sync", lambda s: s.dma_start(out=bkc[:, :], in_=bkcd[:, :]), inc=("sX", 16))
    x_const = C["sX"]

    p0 = contextlib.ExitStack()
    R_ = lambda st, name, shape, dt: st.enter_context(nc.sbuf_tensor(name, shape, dt, side="right"))
    eye_b = R_(p0, "eye_b", [128, 128], BF16)
    bt_b = R_(p0, "bt_b", [128, 3 * NDC * R], BF16)
    xbuf = [R_(p0, f"xbuf{i}", [128, D], F32) for i in range(2)]
    xsq = R_(p0, "xsq", [128, D], F32)
    xntm = [R_(p0, f"xntm{i}", [128, D], BF16) for i in range(2)]
    brow_f = R_(p0, "brow_f", [1, 2 * D], F32)
    ps_tr = [p0.enter_context(nc.psum_tensor(f"ps_tr{i}", [128, 128], BF16)) for i in range(2)]
    ps_misc0 = p0.enter_context(nc.psum_tensor("ps_misc0", [128, 512], F32))

    bl = sch.emit("sync", lambda s: s.dma_start(out=brow_f[:, :], in_=browd.rearrange("a d -> (a d)").rearrange("(o f) -> o f", o=1)), inc=("sX", 16))
    b1 = sch.emit("vector", lambda v: v.tensor_tensor(out=brow_f[0:1, 0:D], in0=brow_f[0:1, 0:D], in1=brow_f[0:1, D:2 * D], op=ALU.add),
                  waits=[("sX", bl)], inc=("sDVE", 1))
    brow_done = sch.emit("vector", lambda v: v.tensor_copy(out=rowpack[32:33, :], in_=brow_f[0:1, 0:D]),
                         waits=[("sDVE", b1), ("sW", w_const)], inc=("sDVE", 1))

    x_done_dve = 0
    tr_evac = [0, 0]
    xntm_free = [0, 0]
    for tt in range(NTT):
        par = tt % 2
        ld = sch.emit("sync", lambda s, tt=tt, par=par: s.dma_start(out=xbuf[par][:, :], in_=xd[tt * 128:(tt + 1) * 128, :]),
                      waits=[("sDVE", x_done_dve)], inc=("sX", 16))
        a = sch.emit("vector", lambda v, par=par: v.reduce_sum(out=stats[:, 0:1], in_=xbuf[par][:, :], axis=AX.X),
                     waits=[("sX", ld)], inc=("sDVE", 1))
        b_ = sch.emit("vector", lambda v, par=par: v.tensor_tensor(out=xsq[:, :], in0=xbuf[par][:, :], in1=xbuf[par][:, :], op=ALU.mult),
                      inc=("sDVE", 1))
        c_ = sch.emit("vector", lambda v: v.tensor_scalar(out=stats[:, 1:2], in0=stats[:, 0:1], scalar1=1.0 / D, scalar2=None, op0=ALU.mult, op1=ALU.bypass),
                      waits=[("sDVE", a)], inc=("sDVE", 1))
        d_ = sch.emit("vector", lambda v: v.reduce_sum(out=stats[:, 2:3], in_=xsq[:, :], axis=AX.X),
                      waits=[("sDVE", b_)], inc=("sDVE", 1))
        e_ = sch.emit("vector", lambda v: v.tensor_tensor(out=stats[:, 3:4], in0=stats[:, 1:2], in1=stats[:, 1:2], op=ALU.mult),
                      waits=[("sDVE", c_)], inc=("sDVE", 1))
        f_ = sch.emit("vector", lambda v: v.tensor_scalar(out=stats[:, 2:3], in0=stats[:, 2:3], scalar1=1.0 / D, scalar2=None, op0=ALU.mult, op1=ALU.bypass),
                      waits=[("sDVE", d_)], inc=("sDVE", 1))
        g_ = sch.emit("vector", lambda v: v.tensor_tensor(out=stats[:, 4:5], in0=stats[:, 2:3], in1=stats[:, 3:4], op=ALU.subtract),
                      waits=[("sDVE", f_)], inc=("sDVE", 1))
        h_ = sch.emit("vector", lambda v: v.tensor_scalar(out=stats[:, 4:5], in0=stats[:, 4:5], scalar1=EPS, scalar2=None, op0=ALU.add, op1=ALU.bypass),
                      waits=[("sDVE", g_)], inc=("sDVE", 1))
        i_ = sch.emit("scalar", lambda s: s.activation(out=stats[:, 5:6], in_=stats[:, 4:5], func=AF.Ln),
                      waits=[("sDVE", h_)], inc=("sACT", 1))
        j_ = sch.emit("scalar", lambda s: s.activation(out=stats[:, 6:7], in_=stats[:, 5:6], func=AF.Exp, scale=-0.5),
                      waits=[("sACT", i_)], inc=("sACT", 1))
        k_ = sch.emit("vector", lambda v, par=par: v.tensor_scalar(out=xntm[par][:, :], in0=xbuf[par][:, :], scalar1=stats[:, 1:2], scalar2=stats[:, 6:7], op0=ALU.subtract, op1=ALU.mult),
                      waits=[("sACT", j_), ("sPE", xntm_free[par])], inc=("sDVE", 1))
        x_done_dve = k_
        for dc in range(NDC):
            tpar = dc % 2
            pe = sch.emit("tensor", lambda t, par=par, dc=dc, tpar=tpar: t.transpose(ps_tr[tpar][:, :], xntm[par][:, dc * 128:(dc + 1) * 128], eye_b[:, :]),
                          waits=[("sDVE", max(k_, tr_evac[tpar])), ("sW", w_const)], inc=("sPE", 1))
            ev = sch.emit("vector", lambda v, tt=tt, dc=dc, tpar=tpar: v.tensor_copy(out=xn_T[:, dc * T + tt * 128: dc * T + (tt + 1) * 128], in_=ps_tr[tpar][:, :]),
                          waits=[("sPE", pe)], inc=("sDVE", 1))
            tr_evac[tpar] = ev
        xntm_free[par] = C["sPE"]
    xn_all = max(tr_evac)

    z_evac = 0
    for i in range(3):
        for tc in range(NTC):
            pe = None
            for dc in range(NDC):
                w = [("sDVE", xn_all), ("sW", w_const)] if dc == 0 else []
                if dc == 0:
                    w.append(("sDVE", z_evac))
                pe = sch.emit("tensor", lambda t, i=i, tc=tc, dc=dc: t.matmul(
                    ps_misc0[0:R, 0:512], bt_b[:, i * NDC * R + dc * R: i * NDC * R + (dc + 1) * R],
                    xn_T[:, dc * T + tc * 512: dc * T + (tc + 1) * 512],
                    start=(dc == 0), stop=(dc == NDC - 1)),
                    waits=w, inc=("sPE", 1) if dc == NDC - 1 else None)
            z_evac = sch.emit("vector", lambda v, i=i, tc=tc: v.tensor_scalar(
                out=zpack[32 * i:32 * i + R, tc * 512:(tc + 1) * 512], in0=ps_misc0[0:R, 0:512],
                scalar1=S4, scalar2=None, op0=ALU.mult, op1=ALU.bypass),
                waits=[("sPE", pe)], inc=("sDVE", 1))
    z_all = z_evac
    bp0 = {s: dict(C)[s] for s in SEMS}
    p0.close()

    p12 = contextlib.ExitStack()
    fscr = [R_(p12, f"fscr{i}", [128, 1024], F32) for i in range(2)]
    fb = [[R_(p12, f"fb{i}_{q}", [128, 1024], BF16) for q in range(4)] for i in range(2)]
    p1 = contextlib.ExitStack()
    O_T = L("O_T", [128, H * T], BF16)
    kbuf = R_(p1, "kbuf", [128, T], BF16)
    qbuf = R_(p1, "qbuf", [128, T], BF16)
    vbuf = R_(p1, "vbuf", [128, NTT * 256], BF16)
    wqb = R_(p1, "wqb", [128, NDC * 128], BF16)
    wkb = R_(p1, "wkb", [128, NDC * 128], BF16)
    wvb = R_(p1, "wvb", [128, NDC * 256], BF16)
    ptb = [R_(p1, f"ptb{i}", [128, 512], BF16) for i in range(2)]
    ps_s = [p1.enter_context(nc.psum_tensor(f"ps_s{i}", [128, 512], F32)) for i in range(2)]
    ps_o = [p1.enter_context(nc.psum_tensor(f"ps_o{i}", [128, 512], F32)) for i in range(2)]
    ps_l = [p1.enter_context(nc.psum_tensor(f"ps_l{i}", [128, 512], F32)) for i in range(2)]
    ps_m = [p1.enter_context(nc.psum_tensor(f"ps_m{i}", [128, 512], F32)) for i in range(2)]

    fstate = dict(chain_sem=[[0, 0], [0, 0]], cast_sem=[0, 0], fb_free=[0, 0], f_ready=[0, 0])

    def emit_fchain256(oc):
        par = oc % 2
        for qp in range(2):
            for r in range(RF):
                for b in range(2):
                    for hh in range(2):
                        q = 2 * qp + b
                        cols = slice(hh * 512, (hh + 1) * 512)
                        if r == 0:
                            fstate["chain_sem"][b][hh] = sch.emit("gpsimd", lambda g, oc=oc, q=q, b=b, cols=cols: g.dma_start(
                                out=fscr[b][:, cols], in_=factd[0, oc, q][:, cols], max_dma_last_dim=8192),
                                waits=[("sDVE", fstate["cast_sem"][b])], inc=("sW", 16))
                        else:
                            fstate["chain_sem"][b][hh] = sch.emit("gpsimd", lambda g, r=r, oc=oc, q=q, b=b, cols=cols: g.dma_start(
                                out=fscr[b][:, cols], in_=factd[r, oc, q][:, cols], accum_op=ALU.add, max_dma_last_dim=8192),
                                waits=[("sW", fstate["chain_sem"][b][hh])], inc=("sW", 16))
            for b in range(2):
                q = 2 * qp + b
                fstate["cast_sem"][b] = sch.emit("vector", lambda v, oc=oc, q=q, par=par, b=b: v.tensor_scalar(
                    out=fb[par][q][:, :], in0=fscr[b][:, :],
                    scalar1=FS, scalar2=None, op0=ALU.mult, op1=ALU.bypass),
                    waits=[("sW", max(fstate["chain_sem"][b])), ("sPE", fstate["fb_free"][par])], inc=("sDVE", 1))
        fstate["f_ready"][par] = max(fstate["cast_sem"])

    st = dict(wq_free=0, wk_free=0, wv_free=0,
              s_free=[0, 0],
              kq_evac=[0, 0],
              ol_free=[0, 0],
              misc_free=[0, 0],
              pt_free=[0, 0],
              qsq_free=[0] * NTC)

    def qk_head(h, which):
        wd_ = wqd if which == 0 else wkd
        wb = wqb if which == 0 else wkb
        bc = bqc if which == 0 else bkc
        dst = qbuf if which == 0 else kbuf
        dma = sch.emit("gpsimd", lambda g, h=h: g.dma_start(
            out=wb[:, :], in_=wd_[h].rearrange("p dc i -> p (dc i)"), max_dma_last_dim=8192),
            waits=[("sPE", max(st["wq_free" if which == 0 else "wk_free"], bp0["sPE"])),
                   ("sDVE", bp0["sDVE"]), ("sACT", bp0["sACT"]), ("sX", bp0["sX"])], inc=("sW", 16))
        last = 0
        for tc in range(NTC):
            par = tc % 2
            for dc in range(NDC):
                w = []
                if dc == 0:
                    w = [("sW", dma), ("sDVE", st["kq_evac"][par]), ("sACT", st["s_free"][par])]
                sch.emit("tensor", lambda t, tc=tc, dc=dc, par=par, wb=wb: t.matmul(
                    ps_s[par][:, :], wb[:, dc * 128:(dc + 1) * 128],
                    xn_T[:, dc * T + tc * 512: dc * T + (tc + 1) * 512],
                    start=(dc == 0), stop=False),
                    waits=w, inc=None)
            af = aq_f if which == 0 else ak_f
            pe = sch.emit("tensor", lambda t, tc=tc, par=par, h=h, af=af: t.matmul(
                ps_s[par][:, :], af[:, h * 128:(h + 1) * 128],
                zpack[:, tc * 512:(tc + 1) * 512],
                start=False, stop=True),
                waits=[("sDVE", z_all)], inc=("sPE", 1))
            last = sch.emit("vector", lambda v, tc=tc, par=par, h=h, dst=dst, bc=bc: v.tensor_scalar(
                out=dst[:, tc * 512:(tc + 1) * 512], in0=ps_s[par][:, :],
                scalar1=bc[:, h:h + 1], scalar2=RT, op0=ALU.add, op1=ALU.mult),
                waits=[("sPE", pe), ("sX", x_const)], inc=("sDVE", 1))
            st["kq_evac"][par] = last
        st["wq_free" if which == 0 else "wk_free"] = C["sPE"]
        return last

    def v_pair(p):
        dma = sch.emit("gpsimd", lambda g, p=p: g.dma_start(
            out=wvb[:, :], in_=wvd[p].rearrange("p dc i -> p (dc i)"), max_dma_last_dim=8192),
            waits=[("sPE", st["wv_free"])], inc=("sW", 16))
        last = 0
        for tt in range(NTT):
            par = tt % 2
            for dc in range(NDC):
                w = []
                if dc == 0:
                    w = [("sW", dma), ("sDVE", st["kq_evac"][par]), ("sACT", st["s_free"][par])]
                sch.emit("tensor", lambda t, tt=tt, dc=dc, par=par: t.matmul(
                    ps_s[par][:, 0:256], xn_T[:, dc * T + tt * 128: dc * T + (tt + 1) * 128],
                    wvb[:, dc * 256:(dc + 1) * 256],
                    start=(dc == 0), stop=False),
                    waits=w, inc=None)
            sch.emit("tensor", lambda t, tt=tt, par=par, p=p: t.matmul(
                ps_s[par][:, 0:256], zpack[:, tt * 128:(tt + 1) * 128],
                av_f[:, p * 256:(p + 1) * 256],
                start=False, stop=False),
                waits=[("sDVE", z_all)], inc=None)
            pe = sch.emit("tensor", lambda t, par=par, p=p: t.matmul(
                ps_s[par][:, 0:256], sel0[:, :], rowpack[:, p * 256:(p + 1) * 256],
                start=False, stop=True),
                waits=[("sW", w_const)], inc=("sPE", 1))
            last = sch.emit("vector", lambda v, tt=tt, par=par: v.tensor_copy(
                out=vbuf[:, tt * 256:(tt + 1) * 256], in_=ps_s[par][:, 0:256]),
                waits=[("sPE", pe)], inc=("sDVE", 1))
            st["kq_evac"][par] = last
        st["wv_free"] = C["sPE"]
        return last

    def c_chain(src, col):
        mx = []
        for qc in range(NTC):
            par = qc % 2
            sq = sch.emit("vector", lambda v, qc=qc, src=src: v.tensor_tensor(
                out=ptb[0][:, :], in0=src[:, qc * 512:(qc + 1) * 512],
                in1=src[:, qc * 512:(qc + 1) * 512], op=ALU.mult),
                waits=[("sPE", max(st["qsq_free"][0], st["pt_free"][0]))], inc=("sDVE", 1))
            pe = sch.emit("tensor", lambda t, qc=qc, par=par: t.matmul(
                ps_m[par][:, :], ones_b[:, :], ptb[0][:, :], start=True, stop=True),
                waits=[("sDVE", max(sq, st["misc_free"][par]))], inc=("sPE", 1))
            st["qsq_free"][0] = pe
            m = sch.emit("vector", lambda v, qc=qc, par=par: v.reduce_max(
                out=mxcol[:, 4 + qc:5 + qc], in_=ps_m[par][:, :], axis=AX.X),
                waits=[("sPE", pe)], inc=("sDVE", 1))
            st["misc_free"][par] = m
            mx.append(m)
        a = sch.emit("vector", lambda v: v.tensor_tensor(out=mxcol[:, 2:3], in0=mxcol[:, 4:5], in1=mxcol[:, 5:6], op=ALU.max),
                     waits=[("sDVE", max(mx))], inc=("sDVE", 1))
        b_ = sch.emit("vector", lambda v: v.tensor_tensor(out=mxcol[:, 3:4], in0=mxcol[:, 6:7], in1=mxcol[:, 7:8], op=ALU.max),
                      inc=("sDVE", 1))
        c_ = sch.emit("vector", lambda v, col=col: v.tensor_tensor(out=mxcol[:, col:col + 1], in0=mxcol[:, 2:3], in1=mxcol[:, 3:4], op=ALU.max),
                      waits=[("sDVE", b_)], inc=("sDVE", 1))
        return c_

    def attn_head(h, hp, ready):
        qn = c_chain(qbuf, 0)
        kn = c_chain(kbuf, 1)
        nch = sch.emit("vector", lambda v, h=h: v.tensor_scalar(
            out=negc[:, h:h + 1], in0=mxcol[:, 0:1], scalar1=mxcol[:, 1:2], scalar2=-0.5, op0=ALU.add, op1=ALU.mult),
            waits=[("sDVE", max(qn, kn))], inc=("sDVE", 1))
        st["pt_free"][0] = max(st["pt_free"][0], st["qsq_free"][0])
        def emit_s_exp(qc, kt):
            spar = kt % 2
            pe_s = sch.emit("tensor", lambda t, qc=qc, kt=kt, spar=spar: t.matmul(
                ps_s[spar][:, :], kbuf[:, kt * 128:(kt + 1) * 128],
                qbuf[:, qc * 512:(qc + 1) * 512], start=True, stop=True),
                waits=[("sDVE", max(ready, st["kq_evac"][spar])), ("sACT", st["s_free"][spar])],
                inc=("sPE", 1))
            ex = sch.emit("scalar", lambda s, h=h, spar=spar: s.activation(
                out=ptb[spar][:, :], in_=ps_s[spar][:, :], func=AF.Exp, bias=negc[:, h:h + 1]),
                waits=[("sPE", max(pe_s, st["pt_free"][spar])), ("sDVE", nch)], inc=("sACT", 1))
            st["s_free"][spar] = ex
            return ex

        for qc in range(NTC):
            opar = qc % 2
            pe_av = 0
            exs = {0: emit_s_exp(qc, 0), 1: emit_s_exp(qc, 1)}
            for kt in range(NTT):
                spar = kt % 2
                w2 = [("sACT", exs[kt])]
                if kt == 0:
                    w2.append(("sDVE", st["ol_free"][opar]))
                sch.emit("tensor", lambda t, kt=kt, spar=spar, opar=opar: t.matmul(
                    ps_l[opar][:, :], ones_b[:, :], ptb[spar][:, :], start=(kt == 0), stop=(kt == NTT - 1)),
                    waits=w2, inc=None)
                pe_av = sch.emit("tensor", lambda t, kt=kt, spar=spar, opar=opar, hp=hp: t.matmul(
                    ps_o[opar][:, :], vbuf[:, kt * 256 + hp * 128: kt * 256 + hp * 128 + 128], ptb[spar][:, :],
                    start=(kt == 0), stop=(kt == NTT - 1)),
                    inc=("sPE", 1))
                st["pt_free"][spar] = pe_av
                if kt + 2 < NTT:
                    exs[kt + 2] = emit_s_exp(qc, kt + 2)
            r_ = sch.emit("vector", lambda v, opar=opar: v.reciprocal(out=fscr[0][:, 0:512], in_=ps_l[opar][:, :]),
                          waits=[("sPE", pe_av)], inc=("sDVE", 1))
            o_ = sch.emit("vector", lambda v, h=h, qc=qc, opar=opar: v.tensor_tensor(
                out=O_T[:, h * T + qc * 512: h * T + (qc + 1) * 512], in0=ps_o[opar][:, :], in1=fscr[0][:, 0:512], op=ALU.mult),
                waits=[("sDVE", r_)], inc=("sDVE", 1))
            st["ol_free"][opar] = o_

    for p in range(H // 2):
        h0, h1 = 2 * p, 2 * p + 1
        q0 = qk_head(h0, 0)
        k0 = qk_head(h0, 1)
        vd = v_pair(p)
        attn_head(h0, 0, max(q0, k0, vd))
        q1 = qk_head(h1, 0)
        k1 = qk_head(h1, 1)
        attn_head(h1, 1, max(q1, k1, vd))
    p1_pe = C["sPE"]
    p1_dve = C["sDVE"]
    bp1 = {s: dict(C)[s] for s in SEMS}
    p1.close()

    p2 = contextlib.ExitStack()
    wob = [R_(p2, f"wob{i}", [128, NDC * 256], BF16) for i in range(2)]
    xsl = [R_(p2, f"xsl{i}", [128, 256], F32) for i in range(2)]
    ostg = [R_(p2, f"ostg{i}", [128, 256], F32) for i in range(2)]
    ps_a = [p2.enter_context(nc.psum_tensor(f"ps_a{i}", [128, 512], F32)) for i in range(2)]
    ps_z = p2.enter_context(nc.psum_tensor("ps_z", [128, 512], F32))

    ao_dma = sch.emit("gpsimd", lambda g: g.dma_start(out=ak_f[32:32 + R, :], in_=aotd[:, :], max_dma_last_dim=8192),
                      waits=[("sPE", p1_pe), ("sDVE", bp1["sDVE"]), ("sACT", bp1["sACT"]), ("sX", bp1["sX"])], inc=("sW", 16))
    zo_done = 0
    for tc in range(NTC):
        pe = None
        for dc in range(NDC):
            w = [("sDVE", max(p1_dve, zo_done)), ("sW", w_const)] if dc == 0 else []
            pe = sch.emit("tensor", lambda t, tc=tc, dc=dc: t.matmul(
                ps_z[0:R, 0:512], bot_b[:, dc * R:(dc + 1) * R],
                O_T[:, dc * T + tc * 512: dc * T + (tc + 1) * 512],
                start=(dc == 0), stop=(dc == NDC - 1)),
                waits=w, inc=("sPE", 1) if dc == NDC - 1 else None)
        zo_done = sch.emit("vector", lambda v, tc=tc: v.tensor_scalar(
            out=zpack[32:32 + R, tc * 512:(tc + 1) * 512], in0=ps_z[0:R, 0:512],
            scalar1=S4, scalar2=None, op0=ALU.mult, op1=ALU.bypass),
            waits=[("sPE", pe)], inc=("sDVE", 1))
    zo_all = zo_done

    p2st = dict(wo_free=[p1_pe, p1_pe], stg_free=[0, 0], x_free=[0, 0], ps_free=[0, 0])
    out_sem = 0
    wo_dmas = {}
    NOC2 = 8

    for oc in range(2):
        wo_dmas[oc] = sch.emit("gpsimd", lambda g, oc=oc: g.dma_start(
            out=wob[oc][:, :], in_=wod[oc].rearrange("p dc i -> p (dc i)"), max_dma_last_dim=8192),
            waits=[("sPE", p1_pe)], inc=("sW", 16))
        emit_fchain256(oc)
    for oc in range(NOC2):
        par = oc % 2
        f_ready = fstate["f_ready"][par]
        for tt in range(NTT):
            tpar = tt % 2
            xl = sch.emit("sync", lambda s, oc=oc, tt=tt, tpar=tpar: s.dma_start(
                out=xsl[tpar][:, :], in_=xd[tt * 128:(tt + 1) * 128, oc * 256:(oc + 1) * 256]),
                waits=[("sDVE", max(p2st["x_free"][tpar], bp1["sDVE"])), ("sPE", bp1["sPE"]), ("sACT", bp1["sACT"])], inc=("sX", 16))
            ps = ps_a[tpar]
            for dc in range(NDC):
                w = [("sW", wo_dmas[oc]), ("sDVE", max(f_ready, p2st["ps_free"][tpar]))] if dc == 0 else []
                sch.emit("tensor", lambda t, tt=tt, dc=dc, ps=ps, par=par: t.matmul(
                    ps[:, 0:256], O_T[:, dc * T + tt * 128: dc * T + (tt + 1) * 128],
                    wob[par][:, dc * 256:(dc + 1) * 256], start=(dc == 0), stop=False),
                    waits=w, inc=None)
            for dc in range(NDC):
                sch.emit("tensor", lambda t, tt=tt, dc=dc, ps=ps, par=par: t.matmul(
                    ps[:, 0:256], xn_T[:, dc * T + tt * 128: dc * T + (tt + 1) * 128],
                    fb[par][dc // 4][:, (dc % 4) * 256:(dc % 4 + 1) * 256], start=False, stop=False),
                    inc=None)
            sch.emit("tensor", lambda t, tt=tt, ps=ps, oc=oc: t.matmul(
                ps[:, 0:256], zpack[:, tt * 128:(tt + 1) * 128],
                ak_f[:, oc * 256:(oc + 1) * 256],
                start=False, stop=False),
                waits=[("sDVE", zo_all), ("sW", ao_dma)], inc=None)
            pe = sch.emit("tensor", lambda t, ps=ps, oc=oc: t.matmul(
                ps[:, 0:256], sel32[:, :], rowpack[:, oc * 256:(oc + 1) * 256],
                start=False, stop=True),
                waits=[("sDVE", brow_done)], inc=("sPE", 1))
            ev = sch.emit("vector", lambda v, tpar=tpar, ps=ps: v.tensor_tensor(
                out=ostg[tpar][:, :], in0=ps[:, 0:256], in1=xsl[tpar][:, :], op=ALU.add),
                waits=[("sPE", pe), ("sX", xl), ("sST", p2st["stg_free"][tpar])], inc=("sDVE", 1))
            p2st["x_free"][tpar] = ev
            p2st["ps_free"][tpar] = ev
            stq = sch.emit("sync", lambda s, oc=oc, tt=tt, tpar=tpar: s.dma_start(
                out=outd[tt * 128:(tt + 1) * 128, oc * 256:(oc + 1) * 256], in_=ostg[tpar][:, :]),
                waits=[("sDVE", ev)], inc=("sST", 16))
            p2st["stg_free"][tpar] = stq
            out_sem = stq
        p2st["wo_free"][par] = C["sPE"]
        fstate["fb_free"][par] = C["sPE"]
        if oc + 2 < NOC2:
            wo_dmas[oc + 2] = sch.emit("gpsimd", lambda g, oc=oc, par=par: g.dma_start(
                out=wob[par][:, :], in_=wod[oc + 2].rearrange("p dc i -> p (dc i)"), max_dma_last_dim=8192),
                waits=[("sPE", p2st["wo_free"][par])], inc=("sW", 16))
            emit_fchain256(oc + 2)

    sch.emit("sync", lambda s: s.nop(), waits=[("sST", out_sem)])
    p2.close()

    with nc.Block() as block:
        @block.sync
        def _(eng):
            sch.replay("sync", eng, sems)

        @block.gpsimd
        def _(eng):
            sch.replay("gpsimd", eng, sems)

        @block.vector
        def _(eng):
            sch.replay("vector", eng, sems)

        @block.scalar
        def _(eng):
            sch.replay("scalar", eng, sems)

        @block.tensor
        def _(eng):
            sch.replay("tensor", eng, sems)

    ctx.close()
    return nc


def _prep_host(inputs):
    g = lambda k: np.asarray(inputs[k], dtype=np.float32)

    def wtile(w, ncol):
        nh = D // ncol
        return np.ascontiguousarray(w.reshape(nh, ncol, NDC, 128).transpose(0, 3, 2, 1))

    def btile(b):
        return np.ascontiguousarray(b.T.reshape(NDC, 128, R).transpose(1, 0, 2).reshape(128, NDC * R))

    m = {}
    m["wq"] = wtile(g("L0_attn_Wq"), 128)
    m["wk"] = wtile(g("L0_attn_Wk"), 128)
    m["wv"] = wtile(g("L0_attn_Wv"), 256)
    m["wo"] = wtile(g("L0_attn_Wo"), 256)
    m["bqc"] = np.ascontiguousarray(g("L0_attn_bq").reshape(H, 128).T)
    m["bkc"] = np.ascontiguousarray(g("L0_attn_bk").reshape(H, 128).T)
    m["bvr"] = np.ascontiguousarray(g("L0_attn_bv").reshape(1, D))
    m["brow"] = np.ascontiguousarray(np.stack([g("L0_attn_bo"), g("L0_ffn_bias")]))
    m["bt"] = np.ascontiguousarray(np.stack([btile(g("L0_attn_Bq")), btile(g("L0_attn_Bk")), btile(g("L0_attn_Bv"))]))
    m["bot"] = btile(g("L0_attn_Bo"))
    m["at"] = np.ascontiguousarray(np.stack([g("L0_attn_Aq").T, g("L0_attn_Ak").T, g("L0_attn_Av").T]))
    m["aot"] = np.ascontiguousarray(g("L0_attn_Ao").T)
    f = g("L0_ffn_factors")
    ft = f.reshape(RF, 4, 4, 128, 8, 256).transpose(0, 4, 1, 3, 2, 5)
    m["fact"] = np.ascontiguousarray(ft.reshape(RF, 8, 4, 128, 1024))
    m["eye"] = np.eye(128, dtype=np.float32)
    return m


def kernel(**inputs):
    nc = build_program()
    shared = _prep_host(inputs)
    x = np.asarray(inputs["L0_attn_x"], dtype=np.float32)
    in_maps = []
    for b in range(B):
        im = dict(shared)
        im["x"] = np.ascontiguousarray(x[b])
        in_maps.append(im)
    res = run_bass_kernel_spmd(nc, in_maps, core_ids=list(range(B)))
    out = np.stack([res.results[b]["out"] for b in range(B)], axis=0)
    return out.astype(np.float32)


if __name__ == "__main__":
    import reference
    inp = reference.setup_inputs()
    out = kernel(**{k: np.asarray(v) for k, v in inp.items()})
    ref = np.asarray(reference.reference(**inp))
    err = float(np.sqrt(np.mean((out - ref) ** 2) / np.mean(ref ** 2)))
    print("rel l2:", err)


# version 26
# speedup vs baseline: 1.0158x; 1.0158x over previous
import numpy as np
import contextlib
import concourse.bass as bass
import concourse.mybir as mybir
from concourse.bass_utils import run_bass_kernel_spmd

F32 = mybir.dt.float32
BF16 = mybir.dt.bfloat16
AF = mybir.ActivationFunctionType
ALU = mybir.AluOpType
AX = mybir.AxisListType

B = 8
T = 2048
D = 2048
H = 16
R = 16
RF = 8
NTT = T // 128
NDC = D // 128
NTC = 4
NOC = 4
RT = float(128 ** -0.25)
S4 = 0.5
FS = float(1.0 / np.sqrt(RF))
EPS = 1e-5

ENGS = ("sync", "gpsimd", "vector", "scalar", "tensor")
SEMS = ("sW", "sX", "sST", "sPE", "sDVE", "sACT", "sG", "sB", "sBr", "sK", "sK2", "sWq", "sWk", "sWv", "sWo0", "sWo1", "sF0", "sF1", "sC0", "sC1", "sX0", "sX1", "sXs0", "sXs1", "sST0", "sST1")


class Sched:

    def __init__(self):
        self.ops = {e: [] for e in ENGS}
        self.cnt = {s: 0 for s in SEMS}
        self.lastwait = {e: {s: 0 for s in SEMS} for e in ENGS}

    def emit(self, eng, builder, waits=(), inc=None):
        ws = []
        for (s, v) in waits:
            if v > self.lastwait[eng][s]:
                ws.append((s, v))
                self.lastwait[eng][s] = v
        post = None
        if inc is not None:
            s, amt = inc
            self.cnt[s] += amt
            post = self.cnt[s]
        self.ops[eng].append((ws, builder, inc))
        return post

    def replay(self, eng_name, eng, sems):
        for ws, builder, inc in self.ops[eng_name]:
            for (s, v) in ws:
                eng.wait_ge(sems[s], v)
            instr = builder(eng)
            if inc is not None:
                s, amt = inc
                instr.then_inc(sems[s], amt)


def build_program():
    nc = bass.Bass("TRN2", target_bir_lowering=False, debug=False)

    xd = nc.dram_tensor("x", [T, D], F32, kind="ExternalInput").ap()
    wqd = nc.dram_tensor("wq", [H, 128, NDC, 128], F32, kind="ExternalInput").ap()
    wkd = nc.dram_tensor("wk", [H, 128, NDC, 128], F32, kind="ExternalInput").ap()
    wvd = nc.dram_tensor("wv", [H // 2, 128, NDC, 256], F32, kind="ExternalInput").ap()
    wod = nc.dram_tensor("wo", [8, 128, NDC, 256], F32, kind="ExternalInput").ap()
    bqcd = nc.dram_tensor("bqc", [128, H], F32, kind="ExternalInput").ap()
    bkcd = nc.dram_tensor("bkc", [128, H], F32, kind="ExternalInput").ap()
    bvrd = nc.dram_tensor("bvr", [1, D], F32, kind="ExternalInput").ap()
    browd = nc.dram_tensor("brow", [2, D], F32, kind="ExternalInput").ap()
    btd = nc.dram_tensor("bt", [3, 128, NDC * R], F32, kind="ExternalInput").ap()
    botd = nc.dram_tensor("bot", [128, NDC * R], F32, kind="ExternalInput").ap()
    atd = nc.dram_tensor("at", [3, R, D], F32, kind="ExternalInput").ap()
    aotd = nc.dram_tensor("aot", [R, D], F32, kind="ExternalInput").ap()
    factd = nc.dram_tensor("fact", [RF, 8, 4, 128, 1024], F32, kind="ExternalInput").ap()
    eyed = nc.dram_tensor("eye", [128, 128], F32, kind="ExternalInput").ap()
    outd = nc.dram_tensor("out", [T, D], F32, kind="ExternalOutput").ap()

    sch = Sched()
    C = sch.cnt

    ctx = contextlib.ExitStack()
    L = lambda name, shape, dt: ctx.enter_context(nc.sbuf_tensor(name, shape, dt, side="left"))

    xn_T = L("xn_T", [128, NDC * T], BF16)
    ones_b = L("ones_b", [128, 128], BF16)
    bqc = L("bqc_s", [128, H], F32)
    bkc = L("bkc_s", [128, H], F32)
    rowpack = L("rowpack", [128, D], BF16)
    sel0 = L("sel0", [128, 128], BF16)
    sel32 = L("sel32", [128, 128], BF16)
    bot_b = L("bot_b", [128, NDC * R], BF16)
    aq_f = L("aq_f", [128, D], BF16)
    ak_f = L("ak_f", [128, D], BF16)
    av_f = L("av_f", [128, D], BF16)
    zpack = L("zpack", [128, T], BF16)
    stats = L("stats", [128, 8], F32)
    negc = L("negc", [128, H], F32)
    mxcol = L("mxcol", [128, 8], F32)

    sems = {s: ctx.enter_context(nc.semaphore(s)) for s in SEMS}

    for fill in (lambda v: v.memset(ones_b[:, :], 1.0),
                 lambda v: v.memset(rowpack[:, :], 0.0),
                 lambda v: v.memset(sel0[:, :], 0.0),
                 lambda v: v.memset(sel32[:, :], 0.0),
                 lambda v: v.memset(aq_f[:, :], 0.0),
                 lambda v: v.memset(ak_f[:, :], 0.0),
                 lambda v: v.memset(av_f[:, :], 0.0),
                 lambda v: v.memset(zpack[:, :], 0.0)):
        sch.emit("vector", fill, inc=("sDVE", 1))
    mset0 = C["sDVE"]
    sch.emit("vector", lambda v: v.memset(sel0[0:1, :], 1.0), waits=[("sDVE", mset0)], inc=("sDVE", 1))
    sch.emit("vector", lambda v: v.memset(sel32[32:33, :], 1.0), inc=("sDVE", 1))
    mset = C["sDVE"]
    sch.emit("gpsimd", lambda g: g.dma_start(out=eye_b[:, :], in_=eyed[:, :]), inc=("sK", 16))
    sch.emit("gpsimd", lambda g: g.dma_start(out=rowpack[0:1, :], in_=bvrd[:, :]),
             waits=[("sDVE", mset)], inc=("sK", 16))
    for i in range(3):
        sch.emit("gpsimd", lambda g, i=i: g.dma_start(out=bt_b[:, i * NDC * R:(i + 1) * NDC * R], in_=btd[i]), inc=("sK", 16))
    sch.emit("gpsimd", lambda g: g.dma_start(out=bot_b[:, :], in_=botd[:, :]), inc=("sK", 16))
    sch.emit("gpsimd", lambda g: g.dma_start(out=aq_f[0:R, :], in_=atd[0]), waits=[("sDVE", mset)], inc=("sK", 16))
    sch.emit("gpsimd", lambda g: g.dma_start(out=ak_f[32:32 + R, :], in_=atd[1]), inc=("sK", 16))
    sch.emit("gpsimd", lambda g: g.dma_start(out=av_f[64:64 + R, :], in_=atd[2]), inc=("sK", 16))
    w_const = C["sK"]
    sch.emit("sync", lambda s: s.dma_start(out=bqc[:, :], in_=bqcd[:, :]), inc=("sB", 16))
    sch.emit("sync", lambda s: s.dma_start(out=bkc[:, :], in_=bkcd[:, :]), inc=("sB", 16))
    x_const = C["sB"]

    p0 = contextlib.ExitStack()
    R_ = lambda st, name, shape, dt: st.enter_context(nc.sbuf_tensor(name, shape, dt, side="right"))
    eye_b = R_(p0, "eye_b", [128, 128], BF16)
    bt_b = R_(p0, "bt_b", [128, 3 * NDC * R], BF16)
    xbuf = [R_(p0, f"xbuf{i}", [128, D], F32) for i in range(2)]
    xsq = R_(p0, "xsq", [128, D], F32)
    xntm = [R_(p0, f"xntm{i}", [128, D], BF16) for i in range(2)]
    brow_f = R_(p0, "brow_f", [1, 2 * D], F32)
    ps_tr = [p0.enter_context(nc.psum_tensor(f"ps_tr{i}", [128, 128], BF16)) for i in range(2)]
    ps_misc0 = p0.enter_context(nc.psum_tensor("ps_misc0", [128, 512], F32))

    bl = sch.emit("sync", lambda s: s.dma_start(out=brow_f[:, :], in_=browd.rearrange("a d -> (a d)").rearrange("(o f) -> o f", o=1)), inc=("sBr", 16))
    b1 = sch.emit("vector", lambda v: v.tensor_tensor(out=brow_f[0:1, 0:D], in0=brow_f[0:1, 0:D], in1=brow_f[0:1, D:2 * D], op=ALU.add),
                  waits=[("sBr", bl)], inc=("sDVE", 1))
    brow_done = sch.emit("vector", lambda v: v.tensor_copy(out=rowpack[32:33, :], in_=brow_f[0:1, 0:D]),
                         waits=[("sDVE", b1), ("sK", w_const)], inc=("sDVE", 1))

    x_done_dve = 0
    tr_evac = [0, 0]
    xntm_free = [0, 0]
    for tt in range(NTT):
        par = tt % 2
        xs = "sX0" if par == 0 else "sX1"
        ld = sch.emit("sync", lambda s, tt=tt, par=par: s.dma_start(out=xbuf[par][:, :], in_=xd[tt * 128:(tt + 1) * 128, :]),
                      waits=[("sDVE", x_done_dve)], inc=(xs, 16))
        a = sch.emit("vector", lambda v, par=par: v.reduce_sum(out=stats[:, 0:1], in_=xbuf[par][:, :], axis=AX.X),
                     waits=[(xs, ld)], inc=("sDVE", 1))
        b_ = sch.emit("vector", lambda v, par=par: v.tensor_tensor(out=xsq[:, :], in0=xbuf[par][:, :], in1=xbuf[par][:, :], op=ALU.mult),
                      inc=("sDVE", 1))
        c_ = sch.emit("vector", lambda v: v.tensor_scalar(out=stats[:, 1:2], in0=stats[:, 0:1], scalar1=1.0 / D, scalar2=None, op0=ALU.mult, op1=ALU.bypass),
                      waits=[("sDVE", a)], inc=("sDVE", 1))
        d_ = sch.emit("vector", lambda v: v.reduce_sum(out=stats[:, 2:3], in_=xsq[:, :], axis=AX.X),
                      waits=[("sDVE", b_)], inc=("sDVE", 1))
        e_ = sch.emit("vector", lambda v: v.tensor_tensor(out=stats[:, 3:4], in0=stats[:, 1:2], in1=stats[:, 1:2], op=ALU.mult),
                      waits=[("sDVE", c_)], inc=("sDVE", 1))
        f_ = sch.emit("vector", lambda v: v.tensor_scalar(out=stats[:, 2:3], in0=stats[:, 2:3], scalar1=1.0 / D, scalar2=None, op0=ALU.mult, op1=ALU.bypass),
                      waits=[("sDVE", d_)], inc=("sDVE", 1))
        g_ = sch.emit("vector", lambda v: v.tensor_tensor(out=stats[:, 4:5], in0=stats[:, 2:3], in1=stats[:, 3:4], op=ALU.subtract),
                      waits=[("sDVE", f_)], inc=("sDVE", 1))
        h_ = sch.emit("vector", lambda v: v.tensor_scalar(out=stats[:, 4:5], in0=stats[:, 4:5], scalar1=EPS, scalar2=None, op0=ALU.add, op1=ALU.bypass),
                      waits=[("sDVE", g_)], inc=("sDVE", 1))
        i_ = sch.emit("scalar", lambda s: s.activation(out=stats[:, 5:6], in_=stats[:, 4:5], func=AF.Ln),
                      waits=[("sDVE", h_)], inc=("sACT", 1))
        j_ = sch.emit("scalar", lambda s: s.activation(out=stats[:, 6:7], in_=stats[:, 5:6], func=AF.Exp, scale=-0.5),
                      waits=[("sACT", i_)], inc=("sACT", 1))
        k_ = sch.emit("vector", lambda v, par=par: v.tensor_scalar(out=xntm[par][:, :], in0=xbuf[par][:, :], scalar1=stats[:, 1:2], scalar2=stats[:, 6:7], op0=ALU.subtract, op1=ALU.mult),
                      waits=[("sACT", j_), ("sPE", xntm_free[par])], inc=("sDVE", 1))
        x_done_dve = k_
        for dc in range(NDC):
            tpar = dc % 2
            pe = sch.emit("tensor", lambda t, par=par, dc=dc, tpar=tpar: t.transpose(ps_tr[tpar][:, :], xntm[par][:, dc * 128:(dc + 1) * 128], eye_b[:, :]),
                          waits=[("sDVE", max(k_, tr_evac[tpar])), ("sK", w_const)], inc=("sPE", 1))
            ev = sch.emit("vector", lambda v, tt=tt, dc=dc, tpar=tpar: v.tensor_copy(out=xn_T[:, dc * T + tt * 128: dc * T + (tt + 1) * 128], in_=ps_tr[tpar][:, :]),
                          waits=[("sPE", pe)], inc=("sDVE", 1))
            tr_evac[tpar] = ev
        xntm_free[par] = C["sPE"]
    xn_all = max(tr_evac)

    z_evac = 0
    for i in range(3):
        for tc in range(NTC):
            pe = None
            for dc in range(NDC):
                w = [("sDVE", xn_all), ("sK", w_const)] if dc == 0 else []
                if dc == 0:
                    w.append(("sDVE", z_evac))
                pe = sch.emit("tensor", lambda t, i=i, tc=tc, dc=dc: t.matmul(
                    ps_misc0[0:R, 0:512], bt_b[:, i * NDC * R + dc * R: i * NDC * R + (dc + 1) * R],
                    xn_T[:, dc * T + tc * 512: dc * T + (tc + 1) * 512],
                    start=(dc == 0), stop=(dc == NDC - 1)),
                    waits=w, inc=("sPE", 1) if dc == NDC - 1 else None)
            z_evac = sch.emit("vector", lambda v, i=i, tc=tc: v.tensor_scalar(
                out=zpack[32 * i:32 * i + R, tc * 512:(tc + 1) * 512], in0=ps_misc0[0:R, 0:512],
                scalar1=S4, scalar2=None, op0=ALU.mult, op1=ALU.bypass),
                waits=[("sPE", pe)], inc=("sDVE", 1))
    z_all = z_evac
    bp0 = {s: dict(C)[s] for s in SEMS}
    p0.close()

    p12 = contextlib.ExitStack()
    fscr = [R_(p12, f"fscr{i}", [128, 1024], F32) for i in range(2)]
    fb = [[R_(p12, f"fb{i}_{q}", [128, 1024], BF16) for q in range(4)] for i in range(2)]
    p1 = contextlib.ExitStack()
    O_T = L("O_T", [128, H * T], BF16)
    kbuf = R_(p1, "kbuf", [128, T], BF16)
    qbuf = R_(p1, "qbuf", [128, T], BF16)
    vbuf = R_(p1, "vbuf", [128, NTT * 256], BF16)
    wqb = R_(p1, "wqb", [128, NDC * 128], BF16)
    wkb = R_(p1, "wkb", [128, NDC * 128], BF16)
    wvb = R_(p1, "wvb", [128, NDC * 256], BF16)
    ptb = [R_(p1, f"ptb{i}", [128, 512], BF16) for i in range(2)]
    ps_s = [p1.enter_context(nc.psum_tensor(f"ps_s{i}", [128, 512], F32)) for i in range(2)]
    ps_o = [p1.enter_context(nc.psum_tensor(f"ps_o{i}", [128, 512], F32)) for i in range(2)]
    ps_l = [p1.enter_context(nc.psum_tensor(f"ps_l{i}", [128, 512], F32)) for i in range(2)]
    ps_m = [p1.enter_context(nc.psum_tensor(f"ps_m{i}", [128, 512], F32)) for i in range(2)]

    fstate = dict(chain_sem=[[0, 0], [0, 0]], cast_sem=[0, 0], fb_free=[0, 0], f_ready=[0, 0])

    def emit_fchain256(oc):
        par = oc % 2
        for qp in range(2):
            for r in range(RF):
                for b in range(2):
                    for hh in range(2):
                        q = 2 * qp + b
                        cols = slice(hh * 512, (hh + 1) * 512)
                        csem = [["sC0", "sF0"], ["sC1", "sF1"]][b][hh]
                        if r == 0:
                            fstate["chain_sem"][b][hh] = sch.emit("gpsimd", lambda g, oc=oc, q=q, b=b, cols=cols: g.dma_start(
                                out=fscr[b][:, cols], in_=factd[0, oc, q][:, cols], max_dma_last_dim=8192),
                                waits=[("sDVE", fstate["cast_sem"][b])], inc=(csem, 16))
                        else:
                            fstate["chain_sem"][b][hh] = sch.emit("gpsimd", lambda g, r=r, oc=oc, q=q, b=b, cols=cols: g.dma_start(
                                out=fscr[b][:, cols], in_=factd[r, oc, q][:, cols], accum_op=ALU.add, max_dma_last_dim=8192),
                                waits=[(csem, fstate["chain_sem"][b][hh])], inc=(csem, 16))
            for b in range(2):
                q = 2 * qp + b
                fstate["cast_sem"][b] = sch.emit("vector", lambda v, oc=oc, q=q, par=par, b=b: v.tensor_scalar(
                    out=fb[par][q][:, :], in0=fscr[b][:, :],
                    scalar1=FS, scalar2=None, op0=ALU.mult, op1=ALU.bypass),
                    waits=[(["sC0", "sC1"][b], fstate["chain_sem"][b][0]),
                           (["sF0", "sF1"][b], fstate["chain_sem"][b][1]),
                           ("sPE", fstate["fb_free"][par])], inc=("sDVE", 1))
        fstate["f_ready"][par] = max(fstate["cast_sem"])

    st = dict(wq_free=0, wk_free=0, wv_free=0,
              s_free=[0, 0],
              kq_evac=[0, 0],
              ol_free=[0, 0],
              misc_free=[0, 0],
              pt_free=[0, 0],
              qsq_free=[0] * NTC)

    def qk_head(h, which):
        wd_ = wqd if which == 0 else wkd
        wb = wqb if which == 0 else wkb
        bc = bqc if which == 0 else bkc
        dst = qbuf if which == 0 else kbuf
        wsem = "sWq" if which == 0 else "sWk"
        dma = sch.emit("gpsimd", lambda g, h=h: g.dma_start(
            out=wb[:, :], in_=wd_[h].rearrange("p dc i -> p (dc i)"), max_dma_last_dim=8192),
            waits=[("sPE", max(st["wq_free" if which == 0 else "wk_free"], bp0["sPE"])),
                   ("sDVE", bp0["sDVE"]), ("sACT", bp0["sACT"]),
                   ("sX0", bp0["sX0"]), ("sX1", bp0["sX1"]), ("sBr", bp0["sBr"])], inc=(wsem, 16))
        last = 0
        for tc in range(NTC):
            par = tc % 2
            for dc in range(NDC):
                w = []
                if dc == 0:
                    w = [(wsem, dma), ("sDVE", st["kq_evac"][par]), ("sACT", st["s_free"][par])]
                sch.emit("tensor", lambda t, tc=tc, dc=dc, par=par, wb=wb: t.matmul(
                    ps_s[par][:, :], wb[:, dc * 128:(dc + 1) * 128],
                    xn_T[:, dc * T + tc * 512: dc * T + (tc + 1) * 512],
                    start=(dc == 0), stop=False),
                    waits=w, inc=None)
            af = aq_f if which == 0 else ak_f
            pe = sch.emit("tensor", lambda t, tc=tc, par=par, h=h, af=af: t.matmul(
                ps_s[par][:, :], af[:, h * 128:(h + 1) * 128],
                zpack[:, tc * 512:(tc + 1) * 512],
                start=False, stop=True),
                waits=[("sDVE", z_all)], inc=("sPE", 1))
            last = sch.emit("vector", lambda v, tc=tc, par=par, h=h, dst=dst, bc=bc: v.tensor_scalar(
                out=dst[:, tc * 512:(tc + 1) * 512], in0=ps_s[par][:, :],
                scalar1=bc[:, h:h + 1], scalar2=RT, op0=ALU.add, op1=ALU.mult),
                waits=[("sPE", pe), ("sB", x_const)], inc=("sDVE", 1))
            st["kq_evac"][par] = last
        st["wq_free" if which == 0 else "wk_free"] = C["sPE"]
        return last

    def v_pair(p):
        dma = sch.emit("gpsimd", lambda g, p=p: g.dma_start(
            out=wvb[:, :], in_=wvd[p].rearrange("p dc i -> p (dc i)"), max_dma_last_dim=8192),
            waits=[("sPE", st["wv_free"])], inc=("sWv", 16))
        last = 0
        for tt in range(NTT):
            par = tt % 2
            for dc in range(NDC):
                w = []
                if dc == 0:
                    w = [("sWv", dma), ("sDVE", st["kq_evac"][par]), ("sACT", st["s_free"][par])]
                sch.emit("tensor", lambda t, tt=tt, dc=dc, par=par: t.matmul(
                    ps_s[par][:, 0:256], xn_T[:, dc * T + tt * 128: dc * T + (tt + 1) * 128],
                    wvb[:, dc * 256:(dc + 1) * 256],
                    start=(dc == 0), stop=False),
                    waits=w, inc=None)
            sch.emit("tensor", lambda t, tt=tt, par=par, p=p: t.matmul(
                ps_s[par][:, 0:256], zpack[:, tt * 128:(tt + 1) * 128],
                av_f[:, p * 256:(p + 1) * 256],
                start=False, stop=False),
                waits=[("sDVE", z_all)], inc=None)
            pe = sch.emit("tensor", lambda t, par=par, p=p: t.matmul(
                ps_s[par][:, 0:256], sel0[:, :], rowpack[:, p * 256:(p + 1) * 256],
                start=False, stop=True),
                waits=[("sK", w_const)], inc=("sPE", 1))
            last = sch.emit("vector", lambda v, tt=tt, par=par: v.tensor_copy(
                out=vbuf[:, tt * 256:(tt + 1) * 256], in_=ps_s[par][:, 0:256]),
                waits=[("sPE", pe)], inc=("sDVE", 1))
            st["kq_evac"][par] = last
        st["wv_free"] = C["sPE"]
        return last

    def c_chain(src, col):
        mx = []
        for qc in range(NTC):
            par = qc % 2
            sq = sch.emit("vector", lambda v, qc=qc, src=src: v.tensor_tensor(
                out=ptb[0][:, :], in0=src[:, qc * 512:(qc + 1) * 512],
                in1=src[:, qc * 512:(qc + 1) * 512], op=ALU.mult),
                waits=[("sPE", max(st["qsq_free"][0], st["pt_free"][0]))], inc=("sDVE", 1))
            pe = sch.emit("tensor", lambda t, qc=qc, par=par: t.matmul(
                ps_m[par][:, :], ones_b[:, :], ptb[0][:, :], start=True, stop=True),
                waits=[("sDVE", max(sq, st["misc_free"][par]))], inc=("sPE", 1))
            st["qsq_free"][0] = pe
            m = sch.emit("vector", lambda v, qc=qc, par=par: v.reduce_max(
                out=mxcol[:, 4 + qc:5 + qc], in_=ps_m[par][:, :], axis=AX.X),
                waits=[("sPE", pe)], inc=("sDVE", 1))
            st["misc_free"][par] = m
            mx.append(m)
        a = sch.emit("vector", lambda v: v.tensor_tensor(out=mxcol[:, 2:3], in0=mxcol[:, 4:5], in1=mxcol[:, 5:6], op=ALU.max),
                     waits=[("sDVE", max(mx))], inc=("sDVE", 1))
        b_ = sch.emit("vector", lambda v: v.tensor_tensor(out=mxcol[:, 3:4], in0=mxcol[:, 6:7], in1=mxcol[:, 7:8], op=ALU.max),
                      inc=("sDVE", 1))
        c_ = sch.emit("vector", lambda v, col=col: v.tensor_tensor(out=mxcol[:, col:col + 1], in0=mxcol[:, 2:3], in1=mxcol[:, 3:4], op=ALU.max),
                      waits=[("sDVE", b_)], inc=("sDVE", 1))
        return c_

    def attn_head(h, hp, ready):
        qn = c_chain(qbuf, 0)
        kn = c_chain(kbuf, 1)
        nch = sch.emit("vector", lambda v, h=h: v.tensor_scalar(
            out=negc[:, h:h + 1], in0=mxcol[:, 0:1], scalar1=mxcol[:, 1:2], scalar2=-0.5, op0=ALU.add, op1=ALU.mult),
            waits=[("sDVE", max(qn, kn))], inc=("sDVE", 1))
        st["pt_free"][0] = max(st["pt_free"][0], st["qsq_free"][0])
        def emit_s_exp(qc, kt):
            spar = kt % 2
            pe_s = sch.emit("tensor", lambda t, qc=qc, kt=kt, spar=spar: t.matmul(
                ps_s[spar][:, :], kbuf[:, kt * 128:(kt + 1) * 128],
                qbuf[:, qc * 512:(qc + 1) * 512], start=True, stop=True),
                waits=[("sDVE", max(ready, st["kq_evac"][spar])), ("sACT", st["s_free"][spar])],
                inc=("sPE", 1))
            ex = sch.emit("scalar", lambda s, h=h, spar=spar: s.activation(
                out=ptb[spar][:, :], in_=ps_s[spar][:, :], func=AF.Exp, bias=negc[:, h:h + 1]),
                waits=[("sPE", max(pe_s, st["pt_free"][spar])), ("sDVE", nch)], inc=("sACT", 1))
            st["s_free"][spar] = ex
            return ex

        for qc in range(NTC):
            opar = qc % 2
            pe_av = 0
            exs = {0: emit_s_exp(qc, 0), 1: emit_s_exp(qc, 1)}
            for kt in range(NTT):
                spar = kt % 2
                w2 = [("sACT", exs[kt])]
                if kt == 0:
                    w2.append(("sDVE", st["ol_free"][opar]))
                sch.emit("tensor", lambda t, kt=kt, spar=spar, opar=opar: t.matmul(
                    ps_l[opar][:, :], ones_b[:, :], ptb[spar][:, :], start=(kt == 0), stop=(kt == NTT - 1)),
                    waits=w2, inc=None)
                pe_av = sch.emit("tensor", lambda t, kt=kt, spar=spar, opar=opar, hp=hp: t.matmul(
                    ps_o[opar][:, :], vbuf[:, kt * 256 + hp * 128: kt * 256 + hp * 128 + 128], ptb[spar][:, :],
                    start=(kt == 0), stop=(kt == NTT - 1)),
                    inc=("sPE", 1))
                st["pt_free"][spar] = pe_av
                if kt + 2 < NTT:
                    exs[kt + 2] = emit_s_exp(qc, kt + 2)
            r_ = sch.emit("vector", lambda v, opar=opar: v.reciprocal(out=fscr[0][:, 0:512], in_=ps_l[opar][:, :]),
                          waits=[("sPE", pe_av)], inc=("sDVE", 1))
            o_ = sch.emit("vector", lambda v, h=h, qc=qc, opar=opar: v.tensor_tensor(
                out=O_T[:, h * T + qc * 512: h * T + (qc + 1) * 512], in0=ps_o[opar][:, :], in1=fscr[0][:, 0:512], op=ALU.mult),
                waits=[("sDVE", r_)], inc=("sDVE", 1))
            st["ol_free"][opar] = o_

    for p in range(H // 2):
        h0, h1 = 2 * p, 2 * p + 1
        q0 = qk_head(h0, 0)
        k0 = qk_head(h0, 1)
        vd = v_pair(p)
        attn_head(h0, 0, max(q0, k0, vd))
        q1 = qk_head(h1, 0)
        k1 = qk_head(h1, 1)
        attn_head(h1, 1, max(q1, k1, vd))
    p1_pe = C["sPE"]
    p1_dve = C["sDVE"]
    bp1 = {s: dict(C)[s] for s in SEMS}
    p1.close()

    p2 = contextlib.ExitStack()
    wob = [R_(p2, f"wob{i}", [128, NDC * 256], BF16) for i in range(2)]
    xsl = [R_(p2, f"xsl{i}", [128, 256], F32) for i in range(2)]
    ostg = [R_(p2, f"ostg{i}", [128, 256], F32) for i in range(2)]
    ps_a = [p2.enter_context(nc.psum_tensor(f"ps_a{i}", [128, 512], F32)) for i in range(2)]
    ps_z = p2.enter_context(nc.psum_tensor("ps_z", [128, 512], F32))

    ao_dma = sch.emit("gpsimd", lambda g: g.dma_start(out=ak_f[32:32 + R, :], in_=aotd[:, :], max_dma_last_dim=8192),
                      waits=[("sPE", p1_pe), ("sDVE", bp1["sDVE"]), ("sACT", bp1["sACT"]),
                             ("sX0", bp1["sX0"]), ("sX1", bp1["sX1"])], inc=("sK2", 16))
    zo_done = 0
    for tc in range(NTC):
        pe = None
        for dc in range(NDC):
            w = [("sDVE", max(p1_dve, zo_done)), ("sK", w_const)] if dc == 0 else []
            pe = sch.emit("tensor", lambda t, tc=tc, dc=dc: t.matmul(
                ps_z[0:R, 0:512], bot_b[:, dc * R:(dc + 1) * R],
                O_T[:, dc * T + tc * 512: dc * T + (tc + 1) * 512],
                start=(dc == 0), stop=(dc == NDC - 1)),
                waits=w, inc=("sPE", 1) if dc == NDC - 1 else None)
        zo_done = sch.emit("vector", lambda v, tc=tc: v.tensor_scalar(
            out=zpack[32:32 + R, tc * 512:(tc + 1) * 512], in0=ps_z[0:R, 0:512],
            scalar1=S4, scalar2=None, op0=ALU.mult, op1=ALU.bypass),
            waits=[("sPE", pe)], inc=("sDVE", 1))
    zo_all = zo_done

    p2st = dict(wo_free=[p1_pe, p1_pe], stg_free=[0, 0], x_free=[0, 0], ps_free=[0, 0])
    out_sem = 0
    wo_dmas = {}
    NOC2 = 8

    for oc in range(2):
        wo_dmas[oc] = sch.emit("gpsimd", lambda g, oc=oc: g.dma_start(
            out=wob[oc][:, :], in_=wod[oc].rearrange("p dc i -> p (dc i)"), max_dma_last_dim=8192),
            waits=[("sPE", p1_pe)], inc=("sWo0" if oc % 2 == 0 else "sWo1", 16))
        emit_fchain256(oc)
    for oc in range(NOC2):
        par = oc % 2
        f_ready = fstate["f_ready"][par]
        for tt in range(NTT):
            tpar = tt % 2
            xl = sch.emit("sync", lambda s, oc=oc, tt=tt, tpar=tpar: s.dma_start(
                out=xsl[tpar][:, :], in_=xd[tt * 128:(tt + 1) * 128, oc * 256:(oc + 1) * 256]),
                waits=[("sDVE", max(p2st["x_free"][tpar], bp1["sDVE"])), ("sPE", bp1["sPE"]), ("sACT", bp1["sACT"])],
                inc=("sXs0" if tpar == 0 else "sXs1", 16))
            ps = ps_a[tpar]
            for dc in range(NDC):
                w = [("sWo0" if par == 0 else "sWo1", wo_dmas[oc]), ("sDVE", max(f_ready, p2st["ps_free"][tpar]))] if dc == 0 else []
                sch.emit("tensor", lambda t, tt=tt, dc=dc, ps=ps, par=par: t.matmul(
                    ps[:, 0:256], O_T[:, dc * T + tt * 128: dc * T + (tt + 1) * 128],
                    wob[par][:, dc * 256:(dc + 1) * 256], start=(dc == 0), stop=False),
                    waits=w, inc=None)
            for dc in range(NDC):
                sch.emit("tensor", lambda t, tt=tt, dc=dc, ps=ps, par=par: t.matmul(
                    ps[:, 0:256], xn_T[:, dc * T + tt * 128: dc * T + (tt + 1) * 128],
                    fb[par][dc // 4][:, (dc % 4) * 256:(dc % 4 + 1) * 256], start=False, stop=False),
                    inc=None)
            sch.emit("tensor", lambda t, tt=tt, ps=ps, oc=oc: t.matmul(
                ps[:, 0:256], zpack[:, tt * 128:(tt + 1) * 128],
                ak_f[:, oc * 256:(oc + 1) * 256],
                start=False, stop=False),
                waits=[("sDVE", zo_all), ("sK2", ao_dma)], inc=None)
            pe = sch.emit("tensor", lambda t, ps=ps, oc=oc: t.matmul(
                ps[:, 0:256], sel32[:, :], rowpack[:, oc * 256:(oc + 1) * 256],
                start=False, stop=True),
                waits=[("sDVE", brow_done)], inc=("sPE", 1))
            ev = sch.emit("vector", lambda v, tpar=tpar, ps=ps: v.tensor_tensor(
                out=ostg[tpar][:, :], in0=ps[:, 0:256], in1=xsl[tpar][:, :], op=ALU.add),
                waits=[("sPE", pe), ("sXs0" if tpar == 0 else "sXs1", xl),
                       ("sST0" if tpar == 0 else "sST1", p2st["stg_free"][tpar])], inc=("sDVE", 1))
            p2st["x_free"][tpar] = ev
            p2st["ps_free"][tpar] = ev
            stq = sch.emit("sync", lambda s, oc=oc, tt=tt, tpar=tpar: s.dma_start(
                out=outd[tt * 128:(tt + 1) * 128, oc * 256:(oc + 1) * 256], in_=ostg[tpar][:, :]),
                waits=[("sDVE", ev)], inc=("sST0" if tpar == 0 else "sST1", 16))
            p2st["stg_free"][tpar] = stq
            out_sem = stq
        p2st["wo_free"][par] = C["sPE"]
        fstate["fb_free"][par] = C["sPE"]
        if oc + 2 < NOC2:
            wo_dmas[oc + 2] = sch.emit("gpsimd", lambda g, oc=oc, par=par: g.dma_start(
                out=wob[par][:, :], in_=wod[oc + 2].rearrange("p dc i -> p (dc i)"), max_dma_last_dim=8192),
                waits=[("sPE", p2st["wo_free"][par])], inc=("sWo0" if par == 0 else "sWo1", 16))
            emit_fchain256(oc + 2)

    sch.emit("sync", lambda s: s.nop(), waits=[("sST0", C["sST0"]), ("sST1", C["sST1"])])
    p2.close()

    with nc.Block() as block:
        @block.sync
        def _(eng):
            sch.replay("sync", eng, sems)

        @block.gpsimd
        def _(eng):
            sch.replay("gpsimd", eng, sems)

        @block.vector
        def _(eng):
            sch.replay("vector", eng, sems)

        @block.scalar
        def _(eng):
            sch.replay("scalar", eng, sems)

        @block.tensor
        def _(eng):
            sch.replay("tensor", eng, sems)

    ctx.close()
    return nc


def _prep_host(inputs):
    g = lambda k: np.asarray(inputs[k], dtype=np.float32)

    def wtile(w, ncol):
        nh = D // ncol
        return np.ascontiguousarray(w.reshape(nh, ncol, NDC, 128).transpose(0, 3, 2, 1))

    def btile(b):
        return np.ascontiguousarray(b.T.reshape(NDC, 128, R).transpose(1, 0, 2).reshape(128, NDC * R))

    m = {}
    m["wq"] = wtile(g("L0_attn_Wq"), 128)
    m["wk"] = wtile(g("L0_attn_Wk"), 128)
    m["wv"] = wtile(g("L0_attn_Wv"), 256)
    m["wo"] = wtile(g("L0_attn_Wo"), 256)
    m["bqc"] = np.ascontiguousarray(g("L0_attn_bq").reshape(H, 128).T)
    m["bkc"] = np.ascontiguousarray(g("L0_attn_bk").reshape(H, 128).T)
    m["bvr"] = np.ascontiguousarray(g("L0_attn_bv").reshape(1, D))
    m["brow"] = np.ascontiguousarray(np.stack([g("L0_attn_bo"), g("L0_ffn_bias")]))
    m["bt"] = np.ascontiguousarray(np.stack([btile(g("L0_attn_Bq")), btile(g("L0_attn_Bk")), btile(g("L0_attn_Bv"))]))
    m["bot"] = btile(g("L0_attn_Bo"))
    m["at"] = np.ascontiguousarray(np.stack([g("L0_attn_Aq").T, g("L0_attn_Ak").T, g("L0_attn_Av").T]))
    m["aot"] = np.ascontiguousarray(g("L0_attn_Ao").T)
    f = g("L0_ffn_factors")
    ft = f.reshape(RF, 4, 4, 128, 8, 256).transpose(0, 4, 1, 3, 2, 5)
    m["fact"] = np.ascontiguousarray(ft.reshape(RF, 8, 4, 128, 1024))
    m["eye"] = np.eye(128, dtype=np.float32)
    return m


def kernel(**inputs):
    nc = build_program()
    shared = _prep_host(inputs)
    x = np.asarray(inputs["L0_attn_x"], dtype=np.float32)
    in_maps = []
    for b in range(B):
        im = dict(shared)
        im["x"] = np.ascontiguousarray(x[b])
        in_maps.append(im)
    res = run_bass_kernel_spmd(nc, in_maps, core_ids=list(range(B)))
    out = np.stack([res.results[b]["out"] for b in range(B)], axis=0)
    return out.astype(np.float32)


if __name__ == "__main__":
    import reference
    inp = reference.setup_inputs()
    out = kernel(**{k: np.asarray(v) for k, v in inp.items()})
    ref = np.asarray(reference.reference(**inp))
    err = float(np.sqrt(np.mean((out - ref) ** 2) / np.mean(ref ** 2)))
    print("rel l2:", err)


# version 28
# speedup vs baseline: 1.0708x; 1.0542x over previous
import numpy as np
import contextlib
import concourse.bass as bass
import concourse.mybir as mybir
from concourse.bass_utils import run_bass_kernel_spmd

F32 = mybir.dt.float32
BF16 = mybir.dt.bfloat16
AF = mybir.ActivationFunctionType
ALU = mybir.AluOpType
AX = mybir.AxisListType

B = 8
T = 2048
D = 2048
H = 16
R = 16
RF = 8
NTT = T // 128
NDC = D // 128
NTC = 4
NOC = 4
RT = float(128 ** -0.25)
S4 = 0.5
FS = float(1.0 / np.sqrt(RF))
EPS = 1e-5

ENGS = ("sync", "gpsimd", "vector", "scalar", "tensor")
SEMS = ("sW", "sX", "sST", "sPE", "sDVE", "sACT", "sG", "sB", "sBr", "sK", "sK2", "sWq", "sWk", "sWv", "sWo0", "sWo1", "sF0", "sF1", "sC0", "sC1", "sX0", "sX1", "sXs0", "sXs1", "sST0", "sST1", "sD0", "sD1", "sE0", "sE1")


class Sched:

    def __init__(self):
        self.ops = {e: [] for e in ENGS}
        self.cnt = {s: 0 for s in SEMS}
        self.lastwait = {e: {s: 0 for s in SEMS} for e in ENGS}

    def emit(self, eng, builder, waits=(), inc=None):
        ws = []
        for (s, v) in waits:
            if v > self.lastwait[eng][s]:
                ws.append((s, v))
                self.lastwait[eng][s] = v
        post = None
        if inc is not None:
            s, amt = inc
            self.cnt[s] += amt
            post = self.cnt[s]
        self.ops[eng].append((ws, builder, inc))
        return post

    def replay(self, eng_name, eng, sems):
        for ws, builder, inc in self.ops[eng_name]:
            for (s, v) in ws:
                eng.wait_ge(sems[s], v)
            instr = builder(eng)
            if inc is not None:
                s, amt = inc
                instr.then_inc(sems[s], amt)


def build_program():
    nc = bass.Bass("TRN2", target_bir_lowering=False, debug=False)

    xd = nc.dram_tensor("x", [T, D], F32, kind="ExternalInput").ap()
    wqd = nc.dram_tensor("wq", [H, 128, NDC, 128], F32, kind="ExternalInput").ap()
    wkd = nc.dram_tensor("wk", [H, 128, NDC, 128], F32, kind="ExternalInput").ap()
    wvd = nc.dram_tensor("wv", [H // 2, 128, NDC, 256], F32, kind="ExternalInput").ap()
    wod = nc.dram_tensor("wo", [8, 128, NDC, 256], F32, kind="ExternalInput").ap()
    bqcd = nc.dram_tensor("bqc", [128, H], F32, kind="ExternalInput").ap()
    bkcd = nc.dram_tensor("bkc", [128, H], F32, kind="ExternalInput").ap()
    bvrd = nc.dram_tensor("bvr", [1, D], F32, kind="ExternalInput").ap()
    browd = nc.dram_tensor("brow", [2, D], F32, kind="ExternalInput").ap()
    btd = nc.dram_tensor("bt", [3, 128, NDC * R], F32, kind="ExternalInput").ap()
    botd = nc.dram_tensor("bot", [128, NDC * R], F32, kind="ExternalInput").ap()
    atd = nc.dram_tensor("at", [3, R, D], F32, kind="ExternalInput").ap()
    aotd = nc.dram_tensor("aot", [R, D], F32, kind="ExternalInput").ap()
    factd = nc.dram_tensor("fact", [RF, 8, 4, 128, 1024], F32, kind="ExternalInput").ap()
    eyed = nc.dram_tensor("eye", [128, 128], F32, kind="ExternalInput").ap()
    outd = nc.dram_tensor("out", [T, D], F32, kind="ExternalOutput").ap()

    sch = Sched()
    C = sch.cnt

    ctx = contextlib.ExitStack()
    L = lambda name, shape, dt: ctx.enter_context(nc.sbuf_tensor(name, shape, dt, side="left"))

    xn_T = L("xn_T", [128, NDC * T], BF16)
    ones_b = L("ones_b", [128, 128], BF16)
    bqc = L("bqc_s", [128, H], F32)
    bkc = L("bkc_s", [128, H], F32)
    rowpack = L("rowpack", [128, D], BF16)
    sel0 = L("sel0", [128, 128], BF16)
    sel32 = L("sel32", [128, 128], BF16)
    bot_b = L("bot_b", [128, NDC * R], BF16)
    aq_f = L("aq_f", [128, D], BF16)
    ak_f = L("ak_f", [128, D], BF16)
    av_f = L("av_f", [128, D], BF16)
    zpack = L("zpack", [128, T], BF16)
    stats = L("stats", [128, 8], F32)
    negc = L("negc", [128, H], F32)
    mxcol = L("mxcol", [128, 8], F32)

    sems = {s: ctx.enter_context(nc.semaphore(s)) for s in SEMS}

    for fill in (lambda v: v.memset(ones_b[:, :], 1.0),
                 lambda v: v.memset(rowpack[:, :], 0.0),
                 lambda v: v.memset(sel0[:, :], 0.0),
                 lambda v: v.memset(sel32[:, :], 0.0),
                 lambda v: v.memset(aq_f[:, :], 0.0),
                 lambda v: v.memset(ak_f[:, :], 0.0),
                 lambda v: v.memset(av_f[:, :], 0.0),
                 lambda v: v.memset(zpack[:, :], 0.0)):
        sch.emit("vector", fill, inc=("sDVE", 1))
    mset0 = C["sDVE"]
    sch.emit("vector", lambda v: v.memset(sel0[0:1, :], 1.0), waits=[("sDVE", mset0)], inc=("sDVE", 1))
    sch.emit("vector", lambda v: v.memset(sel32[32:33, :], 1.0), inc=("sDVE", 1))
    mset = C["sDVE"]
    sch.emit("gpsimd", lambda g: g.dma_start(out=eye_b[:, :], in_=eyed[:, :]), inc=("sK", 16))
    sch.emit("gpsimd", lambda g: g.dma_start(out=rowpack[0:1, :], in_=bvrd[:, :]),
             waits=[("sDVE", mset)], inc=("sK", 16))
    for i in range(3):
        sch.emit("gpsimd", lambda g, i=i: g.dma_start(out=bt_b[:, i * NDC * R:(i + 1) * NDC * R], in_=btd[i]), inc=("sK", 16))
    sch.emit("gpsimd", lambda g: g.dma_start(out=bot_b[:, :], in_=botd[:, :]), inc=("sK", 16))
    sch.emit("gpsimd", lambda g: g.dma_start(out=aq_f[0:R, :], in_=atd[0]), waits=[("sDVE", mset)], inc=("sK", 16))
    sch.emit("gpsimd", lambda g: g.dma_start(out=ak_f[32:32 + R, :], in_=atd[1]), inc=("sK", 16))
    sch.emit("gpsimd", lambda g: g.dma_start(out=av_f[64:64 + R, :], in_=atd[2]), inc=("sK", 16))
    w_const = C["sK"]
    sch.emit("sync", lambda s: s.dma_start(out=bqc[:, :], in_=bqcd[:, :]), inc=("sB", 16))
    sch.emit("sync", lambda s: s.dma_start(out=bkc[:, :], in_=bkcd[:, :]), inc=("sB", 16))
    x_const = C["sB"]

    p0 = contextlib.ExitStack()
    R_ = lambda st, name, shape, dt: st.enter_context(nc.sbuf_tensor(name, shape, dt, side="right"))
    eye_b = R_(p0, "eye_b", [128, 128], BF16)
    bt_b = R_(p0, "bt_b", [128, 3 * NDC * R], BF16)
    xbuf = [R_(p0, f"xbuf{i}", [128, D], F32) for i in range(2)]
    xsq = R_(p0, "xsq", [128, D], F32)
    xntm = [R_(p0, f"xntm{i}", [128, D], BF16) for i in range(2)]
    brow_f = R_(p0, "brow_f", [1, 2 * D], F32)
    ps_tr = [p0.enter_context(nc.psum_tensor(f"ps_tr{i}", [128, 128], BF16)) for i in range(2)]
    ps_misc0 = p0.enter_context(nc.psum_tensor("ps_misc0", [128, 512], F32))

    bl = sch.emit("sync", lambda s: s.dma_start(out=brow_f[:, :], in_=browd.rearrange("a d -> (a d)").rearrange("(o f) -> o f", o=1)), inc=("sBr", 16))
    b1 = sch.emit("vector", lambda v: v.tensor_tensor(out=brow_f[0:1, 0:D], in0=brow_f[0:1, 0:D], in1=brow_f[0:1, D:2 * D], op=ALU.add),
                  waits=[("sBr", bl)], inc=("sDVE", 1))
    brow_done = sch.emit("vector", lambda v: v.tensor_copy(out=rowpack[32:33, :], in_=brow_f[0:1, 0:D]),
                         waits=[("sDVE", b1), ("sK", w_const)], inc=("sDVE", 1))

    x_done_dve = 0
    tr_evac = [0, 0]
    xntm_free = [0, 0]
    for tt in range(NTT):
        par = tt % 2
        xs = "sX0" if par == 0 else "sX1"
        ld = sch.emit("sync", lambda s, tt=tt, par=par: s.dma_start(out=xbuf[par][:, :], in_=xd[tt * 128:(tt + 1) * 128, :]),
                      waits=[("sDVE", x_done_dve)], inc=(xs, 16))
        a = sch.emit("vector", lambda v, par=par: v.reduce_sum(out=stats[:, 0:1], in_=xbuf[par][:, :], axis=AX.X),
                     waits=[(xs, ld)], inc=("sDVE", 1))
        b_ = sch.emit("vector", lambda v, par=par: v.tensor_tensor(out=xsq[:, :], in0=xbuf[par][:, :], in1=xbuf[par][:, :], op=ALU.mult),
                      inc=("sDVE", 1))
        c_ = sch.emit("vector", lambda v: v.tensor_scalar(out=stats[:, 1:2], in0=stats[:, 0:1], scalar1=1.0 / D, scalar2=None, op0=ALU.mult, op1=ALU.bypass),
                      waits=[("sDVE", a)], inc=("sDVE", 1))
        d_ = sch.emit("vector", lambda v: v.reduce_sum(out=stats[:, 2:3], in_=xsq[:, :], axis=AX.X),
                      waits=[("sDVE", b_)], inc=("sDVE", 1))
        e_ = sch.emit("vector", lambda v: v.tensor_tensor(out=stats[:, 3:4], in0=stats[:, 1:2], in1=stats[:, 1:2], op=ALU.mult),
                      waits=[("sDVE", c_)], inc=("sDVE", 1))
        f_ = sch.emit("vector", lambda v: v.tensor_scalar(out=stats[:, 2:3], in0=stats[:, 2:3], scalar1=1.0 / D, scalar2=None, op0=ALU.mult, op1=ALU.bypass),
                      waits=[("sDVE", d_)], inc=("sDVE", 1))
        g_ = sch.emit("vector", lambda v: v.tensor_tensor(out=stats[:, 4:5], in0=stats[:, 2:3], in1=stats[:, 3:4], op=ALU.subtract),
                      waits=[("sDVE", f_)], inc=("sDVE", 1))
        h_ = sch.emit("vector", lambda v: v.tensor_scalar(out=stats[:, 4:5], in0=stats[:, 4:5], scalar1=EPS, scalar2=None, op0=ALU.add, op1=ALU.bypass),
                      waits=[("sDVE", g_)], inc=("sDVE", 1))
        i_ = sch.emit("scalar", lambda s: s.activation(out=stats[:, 5:6], in_=stats[:, 4:5], func=AF.Ln),
                      waits=[("sDVE", h_)], inc=("sACT", 1))
        j_ = sch.emit("scalar", lambda s: s.activation(out=stats[:, 6:7], in_=stats[:, 5:6], func=AF.Exp, scale=-0.5),
                      waits=[("sACT", i_)], inc=("sACT", 1))
        k_ = sch.emit("vector", lambda v, par=par: v.tensor_scalar(out=xntm[par][:, :], in0=xbuf[par][:, :], scalar1=stats[:, 1:2], scalar2=stats[:, 6:7], op0=ALU.subtract, op1=ALU.mult),
                      waits=[("sACT", j_), ("sPE", xntm_free[par])], inc=("sDVE", 1))
        x_done_dve = k_
        for dc in range(NDC):
            tpar = dc % 2
            pe = sch.emit("tensor", lambda t, par=par, dc=dc, tpar=tpar: t.transpose(ps_tr[tpar][:, :], xntm[par][:, dc * 128:(dc + 1) * 128], eye_b[:, :]),
                          waits=[("sDVE", max(k_, tr_evac[tpar])), ("sK", w_const)], inc=("sPE", 1))
            ev = sch.emit("vector", lambda v, tt=tt, dc=dc, tpar=tpar: v.tensor_copy(out=xn_T[:, dc * T + tt * 128: dc * T + (tt + 1) * 128], in_=ps_tr[tpar][:, :]),
                          waits=[("sPE", pe)], inc=("sDVE", 1))
            tr_evac[tpar] = ev
        xntm_free[par] = C["sPE"]
    xn_all = max(tr_evac)

    z_evac = 0
    for i in range(3):
        for tc in range(NTC):
            pe = None
            for dc in range(NDC):
                w = [("sDVE", xn_all), ("sK", w_const)] if dc == 0 else []
                if dc == 0:
                    w.append(("sDVE", z_evac))
                pe = sch.emit("tensor", lambda t, i=i, tc=tc, dc=dc: t.matmul(
                    ps_misc0[0:R, 0:512], bt_b[:, i * NDC * R + dc * R: i * NDC * R + (dc + 1) * R],
                    xn_T[:, dc * T + tc * 512: dc * T + (tc + 1) * 512],
                    start=(dc == 0), stop=(dc == NDC - 1)),
                    waits=w, inc=("sPE", 1) if dc == NDC - 1 else None)
            z_evac = sch.emit("vector", lambda v, i=i, tc=tc: v.tensor_scalar(
                out=zpack[32 * i:32 * i + R, tc * 512:(tc + 1) * 512], in0=ps_misc0[0:R, 0:512],
                scalar1=S4, scalar2=None, op0=ALU.mult, op1=ALU.bypass),
                waits=[("sPE", pe)], inc=("sDVE", 1))
    z_all = z_evac
    bp0 = {s: dict(C)[s] for s in SEMS}
    p0.close()

    p12 = contextlib.ExitStack()
    fscr = [R_(p12, f"fscr{i}", [128, 1024], F32) for i in range(2)]
    fb = [[R_(p12, f"fb{i}_{q}", [128, 1024], BF16) for q in range(4)] for i in range(2)]
    p1 = contextlib.ExitStack()
    O_T = L("O_T", [128, H * T], BF16)
    kbuf = R_(p1, "kbuf", [128, T], BF16)
    qbuf = R_(p1, "qbuf", [128, T], BF16)
    vbuf = R_(p1, "vbuf", [128, NTT * 256], BF16)
    wqb = R_(p1, "wqb", [128, NDC * 128], BF16)
    wkb = R_(p1, "wkb", [128, NDC * 128], BF16)
    wvb = R_(p1, "wvb", [128, NDC * 256], BF16)
    ptb = [R_(p1, f"ptb{i}", [128, 512], BF16) for i in range(2)]
    ps_s = [p1.enter_context(nc.psum_tensor(f"ps_s{i}", [128, 512], F32)) for i in range(2)]
    ps_o = [p1.enter_context(nc.psum_tensor(f"ps_o{i}", [128, 512], F32)) for i in range(2)]
    ps_l = [p1.enter_context(nc.psum_tensor(f"ps_l{i}", [128, 512], F32)) for i in range(2)]
    ps_m = [p1.enter_context(nc.psum_tensor(f"ps_m{i}", [128, 512], F32)) for i in range(2)]

    fstate = dict(chain_sem=[[[0, 0], [0, 0]], [[0, 0], [0, 0]]], cast_sem=[0, 0], fb_free=[0, 0], f_ready=[0, 0])
    FSEM = [[["sC0", "sF0"], ["sC1", "sF1"]], [["sD0", "sE0"], ["sD1", "sE1"]]]
    FBUF = [fscr, None]

    def emit_fchain256(oc):
        par = oc % 2
        for qp in range(2):
            for s in range(RF // 2):
                for g in range(2):
                    for b in range(2):
                        for hh in range(2):
                            q = 2 * qp + b
                            r = 4 * g + s
                            cols = slice(hh * 512, (hh + 1) * 512)
                            csem = FSEM[g][b][hh]
                            if s == 0:
                                fstate["chain_sem"][g][b][hh] = sch.emit("gpsimd", lambda gg, r=r, oc=oc, q=q, g=g, b=b, cols=cols: gg.dma_start(
                                    out=FBUF[g][b][:, cols], in_=factd[r, oc, q][:, cols], max_dma_last_dim=8192),
                                    waits=[("sDVE", fstate["cast_sem"][b])], inc=(csem, 16))
                            else:
                                fstate["chain_sem"][g][b][hh] = sch.emit("gpsimd", lambda gg, r=r, oc=oc, q=q, g=g, b=b, cols=cols: gg.dma_start(
                                    out=FBUF[g][b][:, cols], in_=factd[r, oc, q][:, cols], accum_op=ALU.add, max_dma_last_dim=8192),
                                    waits=[(csem, fstate["chain_sem"][g][b][hh])], inc=(csem, 16))
            for b in range(2):
                q = 2 * qp + b
                cadd = sch.emit("vector", lambda v, b=b: v.tensor_tensor(
                    out=fscr[b][:, :], in0=fscr[b][:, :], in1=fscr2[b][:, :], op=ALU.add),
                    waits=[(FSEM[0][b][0], fstate["chain_sem"][0][b][0]),
                           (FSEM[0][b][1], fstate["chain_sem"][0][b][1]),
                           (FSEM[1][b][0], fstate["chain_sem"][1][b][0]),
                           (FSEM[1][b][1], fstate["chain_sem"][1][b][1])], inc=("sDVE", 1))
                fstate["cast_sem"][b] = sch.emit("vector", lambda v, oc=oc, q=q, par=par, b=b: v.tensor_scalar(
                    out=fb[par][q][:, :], in0=fscr[b][:, :],
                    scalar1=FS, scalar2=None, op0=ALU.mult, op1=ALU.bypass),
                    waits=[("sDVE", cadd), ("sPE", fstate["fb_free"][par])], inc=("sDVE", 1))
        fstate["f_ready"][par] = max(fstate["cast_sem"])

    st = dict(wq_free=0, wk_free=0, wv_free=0,
              s_free=[0, 0],
              kq_evac=[0, 0],
              ol_free=[0, 0],
              misc_free=[0, 0],
              pt_free=[0, 0],
              qsq_free=[0] * NTC)

    def qk_head(h, which):
        wd_ = wqd if which == 0 else wkd
        wb = wqb if which == 0 else wkb
        bc = bqc if which == 0 else bkc
        dst = qbuf if which == 0 else kbuf
        wsem = "sWq" if which == 0 else "sWk"
        dma = sch.emit("gpsimd", lambda g, h=h: g.dma_start(
            out=wb[:, :], in_=wd_[h].rearrange("p dc i -> p (dc i)"), max_dma_last_dim=8192),
            waits=[("sPE", max(st["wq_free" if which == 0 else "wk_free"], bp0["sPE"])),
                   ("sDVE", bp0["sDVE"]), ("sACT", bp0["sACT"]),
                   ("sX0", bp0["sX0"]), ("sX1", bp0["sX1"]), ("sBr", bp0["sBr"])], inc=(wsem, 16))
        last = 0
        for tc in range(NTC):
            par = tc % 2
            for dc in range(NDC):
                w = []
                if dc == 0:
                    w = [(wsem, dma), ("sDVE", st["kq_evac"][par]), ("sACT", st["s_free"][par])]
                sch.emit("tensor", lambda t, tc=tc, dc=dc, par=par, wb=wb: t.matmul(
                    ps_s[par][:, :], wb[:, dc * 128:(dc + 1) * 128],
                    xn_T[:, dc * T + tc * 512: dc * T + (tc + 1) * 512],
                    start=(dc == 0), stop=False),
                    waits=w, inc=None)
            af = aq_f if which == 0 else ak_f
            pe = sch.emit("tensor", lambda t, tc=tc, par=par, h=h, af=af: t.matmul(
                ps_s[par][:, :], af[:, h * 128:(h + 1) * 128],
                zpack[:, tc * 512:(tc + 1) * 512],
                start=False, stop=True),
                waits=[("sDVE", z_all)], inc=("sPE", 1))
            last = sch.emit("vector", lambda v, tc=tc, par=par, h=h, dst=dst, bc=bc: v.tensor_scalar(
                out=dst[:, tc * 512:(tc + 1) * 512], in0=ps_s[par][:, :],
                scalar1=bc[:, h:h + 1], scalar2=RT, op0=ALU.add, op1=ALU.mult),
                waits=[("sPE", pe), ("sB", x_const)], inc=("sDVE", 1))
            st["kq_evac"][par] = last
        st["wq_free" if which == 0 else "wk_free"] = C["sPE"]
        return last

    def v_pair(p):
        dma = sch.emit("gpsimd", lambda g, p=p: g.dma_start(
            out=wvb[:, :], in_=wvd[p].rearrange("p dc i -> p (dc i)"), max_dma_last_dim=8192),
            waits=[("sPE", st["wv_free"])], inc=("sWv", 16))
        last = 0
        for tt in range(NTT):
            par = tt % 2
            for dc in range(NDC):
                w = []
                if dc == 0:
                    w = [("sWv", dma), ("sDVE", st["kq_evac"][par]), ("sACT", st["s_free"][par])]
                sch.emit("tensor", lambda t, tt=tt, dc=dc, par=par: t.matmul(
                    ps_s[par][:, 0:256], xn_T[:, dc * T + tt * 128: dc * T + (tt + 1) * 128],
                    wvb[:, dc * 256:(dc + 1) * 256],
                    start=(dc == 0), stop=False),
                    waits=w, inc=None)
            sch.emit("tensor", lambda t, tt=tt, par=par, p=p: t.matmul(
                ps_s[par][:, 0:256], zpack[:, tt * 128:(tt + 1) * 128],
                av_f[:, p * 256:(p + 1) * 256],
                start=False, stop=False),
                waits=[("sDVE", z_all)], inc=None)
            pe = sch.emit("tensor", lambda t, par=par, p=p: t.matmul(
                ps_s[par][:, 0:256], sel0[:, :], rowpack[:, p * 256:(p + 1) * 256],
                start=False, stop=True),
                waits=[("sK", w_const)], inc=("sPE", 1))
            last = sch.emit("vector", lambda v, tt=tt, par=par: v.tensor_copy(
                out=vbuf[:, tt * 256:(tt + 1) * 256], in_=ps_s[par][:, 0:256]),
                waits=[("sPE", pe)], inc=("sDVE", 1))
            st["kq_evac"][par] = last
        st["wv_free"] = C["sPE"]
        return last

    def c_chain(src, col):
        mx = []
        for qc in range(NTC):
            par = qc % 2
            sq = sch.emit("vector", lambda v, qc=qc, src=src: v.tensor_tensor(
                out=ptb[0][:, :], in0=src[:, qc * 512:(qc + 1) * 512],
                in1=src[:, qc * 512:(qc + 1) * 512], op=ALU.mult),
                waits=[("sPE", max(st["qsq_free"][0], st["pt_free"][0]))], inc=("sDVE", 1))
            pe = sch.emit("tensor", lambda t, qc=qc, par=par: t.matmul(
                ps_m[par][:, :], ones_b[:, :], ptb[0][:, :], start=True, stop=True),
                waits=[("sDVE", max(sq, st["misc_free"][par]))], inc=("sPE", 1))
            st["qsq_free"][0] = pe
            m = sch.emit("vector", lambda v, qc=qc, par=par: v.reduce_max(
                out=mxcol[:, 4 + qc:5 + qc], in_=ps_m[par][:, :], axis=AX.X),
                waits=[("sPE", pe)], inc=("sDVE", 1))
            st["misc_free"][par] = m
            mx.append(m)
        a = sch.emit("vector", lambda v: v.tensor_tensor(out=mxcol[:, 2:3], in0=mxcol[:, 4:5], in1=mxcol[:, 5:6], op=ALU.max),
                     waits=[("sDVE", max(mx))], inc=("sDVE", 1))
        b_ = sch.emit("vector", lambda v: v.tensor_tensor(out=mxcol[:, 3:4], in0=mxcol[:, 6:7], in1=mxcol[:, 7:8], op=ALU.max),
                      inc=("sDVE", 1))
        c_ = sch.emit("vector", lambda v, col=col: v.tensor_tensor(out=mxcol[:, col:col + 1], in0=mxcol[:, 2:3], in1=mxcol[:, 3:4], op=ALU.max),
                      waits=[("sDVE", b_)], inc=("sDVE", 1))
        return c_

    def attn_head(h, hp, ready):
        qn = c_chain(qbuf, 0)
        kn = c_chain(kbuf, 1)
        nch = sch.emit("vector", lambda v, h=h: v.tensor_scalar(
            out=negc[:, h:h + 1], in0=mxcol[:, 0:1], scalar1=mxcol[:, 1:2], scalar2=-0.5, op0=ALU.add, op1=ALU.mult),
            waits=[("sDVE", max(qn, kn))], inc=("sDVE", 1))
        st["pt_free"][0] = max(st["pt_free"][0], st["qsq_free"][0])
        def emit_s_exp(qc, kt):
            spar = kt % 2
            pe_s = sch.emit("tensor", lambda t, qc=qc, kt=kt, spar=spar: t.matmul(
                ps_s[spar][:, :], kbuf[:, kt * 128:(kt + 1) * 128],
                qbuf[:, qc * 512:(qc + 1) * 512], start=True, stop=True),
                waits=[("sDVE", max(ready, st["kq_evac"][spar])), ("sACT", st["s_free"][spar])],
                inc=("sPE", 1))
            ex = sch.emit("scalar", lambda s, h=h, spar=spar: s.activation(
                out=ptb[spar][:, :], in_=ps_s[spar][:, :], func=AF.Exp, bias=negc[:, h:h + 1]),
                waits=[("sPE", max(pe_s, st["pt_free"][spar])), ("sDVE", nch)], inc=("sACT", 1))
            st["s_free"][spar] = ex
            return ex

        for qc in range(NTC):
            opar = qc % 2
            pe_av = 0
            exs = {0: emit_s_exp(qc, 0), 1: emit_s_exp(qc, 1)}
            for kt in range(NTT):
                spar = kt % 2
                w2 = [("sACT", exs[kt])]
                if kt == 0:
                    w2.append(("sDVE", st["ol_free"][opar]))
                sch.emit("tensor", lambda t, kt=kt, spar=spar, opar=opar: t.matmul(
                    ps_l[opar][:, :], ones_b[:, :], ptb[spar][:, :], start=(kt == 0), stop=(kt == NTT - 1)),
                    waits=w2, inc=None)
                pe_av = sch.emit("tensor", lambda t, kt=kt, spar=spar, opar=opar, hp=hp: t.matmul(
                    ps_o[opar][:, :], vbuf[:, kt * 256 + hp * 128: kt * 256 + hp * 128 + 128], ptb[spar][:, :],
                    start=(kt == 0), stop=(kt == NTT - 1)),
                    inc=("sPE", 1))
                st["pt_free"][spar] = pe_av
                if kt + 2 < NTT:
                    exs[kt + 2] = emit_s_exp(qc, kt + 2)
            r_ = sch.emit("vector", lambda v, opar=opar: v.reciprocal(out=fscr[0][:, 0:512], in_=ps_l[opar][:, :]),
                          waits=[("sPE", pe_av)], inc=("sDVE", 1))
            o_ = sch.emit("vector", lambda v, h=h, qc=qc, opar=opar: v.tensor_tensor(
                out=O_T[:, h * T + qc * 512: h * T + (qc + 1) * 512], in0=ps_o[opar][:, :], in1=fscr[0][:, 0:512], op=ALU.mult),
                waits=[("sDVE", r_)], inc=("sDVE", 1))
            st["ol_free"][opar] = o_

    for p in range(H // 2):
        h0, h1 = 2 * p, 2 * p + 1
        q0 = qk_head(h0, 0)
        k0 = qk_head(h0, 1)
        vd = v_pair(p)
        attn_head(h0, 0, max(q0, k0, vd))
        q1 = qk_head(h1, 0)
        k1 = qk_head(h1, 1)
        attn_head(h1, 1, max(q1, k1, vd))
    p1_pe = C["sPE"]
    p1_dve = C["sDVE"]
    bp1 = {s: dict(C)[s] for s in SEMS}
    p1.close()

    p2 = contextlib.ExitStack()
    wob = [R_(p2, f"wob{i}", [128, NDC * 256], BF16) for i in range(2)]
    xsl = [R_(p2, f"xsl{i}", [128, 256], F32) for i in range(2)]
    ostg = [R_(p2, f"ostg{i}", [128, 256], F32) for i in range(2)]
    ps_a = [p2.enter_context(nc.psum_tensor(f"ps_a{i}", [128, 512], F32)) for i in range(2)]
    ps_z = p2.enter_context(nc.psum_tensor("ps_z", [128, 512], F32))
    fscr2 = [R_(p2, f"fscr2{i}", [128, 1024], F32) for i in range(2)]
    FBUF[1] = fscr2

    ao_dma = sch.emit("gpsimd", lambda g: g.dma_start(out=ak_f[32:32 + R, :], in_=aotd[:, :], max_dma_last_dim=8192),
                      waits=[("sPE", p1_pe), ("sDVE", bp1["sDVE"]), ("sACT", bp1["sACT"]),
                             ("sX0", bp1["sX0"]), ("sX1", bp1["sX1"])], inc=("sK2", 16))
    zo_done = 0
    for tc in range(NTC):
        pe = None
        for dc in range(NDC):
            w = [("sDVE", max(p1_dve, zo_done)), ("sK", w_const)] if dc == 0 else []
            pe = sch.emit("tensor", lambda t, tc=tc, dc=dc: t.matmul(
                ps_z[0:R, 0:512], bot_b[:, dc * R:(dc + 1) * R],
                O_T[:, dc * T + tc * 512: dc * T + (tc + 1) * 512],
                start=(dc == 0), stop=(dc == NDC - 1)),
                waits=w, inc=("sPE", 1) if dc == NDC - 1 else None)
        zo_done = sch.emit("vector", lambda v, tc=tc: v.tensor_scalar(
            out=zpack[32:32 + R, tc * 512:(tc + 1) * 512], in0=ps_z[0:R, 0:512],
            scalar1=S4, scalar2=None, op0=ALU.mult, op1=ALU.bypass),
            waits=[("sPE", pe)], inc=("sDVE", 1))
    zo_all = zo_done

    p2st = dict(wo_free=[p1_pe, p1_pe], stg_free=[0, 0], x_free=[0, 0], ps_free=[0, 0])
    out_sem = 0
    wo_dmas = {}
    NOC2 = 8

    for oc in range(2):
        wo_dmas[oc] = sch.emit("gpsimd", lambda g, oc=oc: g.dma_start(
            out=wob[oc][:, :], in_=wod[oc].rearrange("p dc i -> p (dc i)"), max_dma_last_dim=8192),
            waits=[("sPE", p1_pe)], inc=("sWo0" if oc % 2 == 0 else "sWo1", 16))
        emit_fchain256(oc)
    for oc in range(NOC2):
        par = oc % 2
        f_ready = fstate["f_ready"][par]
        for tt in range(NTT):
            tpar = tt % 2
            xl = sch.emit("sync", lambda s, oc=oc, tt=tt, tpar=tpar: s.dma_start(
                out=xsl[tpar][:, :], in_=xd[tt * 128:(tt + 1) * 128, oc * 256:(oc + 1) * 256]),
                waits=[("sDVE", max(p2st["x_free"][tpar], bp1["sDVE"])), ("sPE", bp1["sPE"]), ("sACT", bp1["sACT"])],
                inc=("sXs0" if tpar == 0 else "sXs1", 16))
            ps = ps_a[tpar]
            for dc in range(NDC):
                w = [("sWo0" if par == 0 else "sWo1", wo_dmas[oc]), ("sDVE", max(f_ready, p2st["ps_free"][tpar]))] if dc == 0 else []
                sch.emit("tensor", lambda t, tt=tt, dc=dc, ps=ps, par=par: t.matmul(
                    ps[:, 0:256], O_T[:, dc * T + tt * 128: dc * T + (tt + 1) * 128],
                    wob[par][:, dc * 256:(dc + 1) * 256], start=(dc == 0), stop=False),
                    waits=w, inc=None)
            for dc in range(NDC):
                sch.emit("tensor", lambda t, tt=tt, dc=dc, ps=ps, par=par: t.matmul(
                    ps[:, 0:256], xn_T[:, dc * T + tt * 128: dc * T + (tt + 1) * 128],
                    fb[par][dc // 4][:, (dc % 4) * 256:(dc % 4 + 1) * 256], start=False, stop=False),
                    inc=None)
            sch.emit("tensor", lambda t, tt=tt, ps=ps, oc=oc: t.matmul(
                ps[:, 0:256], zpack[:, tt * 128:(tt + 1) * 128],
                ak_f[:, oc * 256:(oc + 1) * 256],
                start=False, stop=False),
                waits=[("sDVE", zo_all), ("sK2", ao_dma)], inc=None)
            pe = sch.emit("tensor", lambda t, ps=ps, oc=oc: t.matmul(
                ps[:, 0:256], sel32[:, :], rowpack[:, oc * 256:(oc + 1) * 256],
                start=False, stop=True),
                waits=[("sDVE", brow_done)], inc=("sPE", 1))
            ev = sch.emit("vector", lambda v, tpar=tpar, ps=ps: v.tensor_tensor(
                out=ostg[tpar][:, :], in0=ps[:, 0:256], in1=xsl[tpar][:, :], op=ALU.add),
                waits=[("sPE", pe), ("sXs0" if tpar == 0 else "sXs1", xl),
                       ("sST0" if tpar == 0 else "sST1", p2st["stg_free"][tpar])], inc=("sDVE", 1))
            p2st["x_free"][tpar] = ev
            p2st["ps_free"][tpar] = ev
            stq = sch.emit("sync", lambda s, oc=oc, tt=tt, tpar=tpar: s.dma_start(
                out=outd[tt * 128:(tt + 1) * 128, oc * 256:(oc + 1) * 256], in_=ostg[tpar][:, :]),
                waits=[("sDVE", ev)], inc=("sST0" if tpar == 0 else "sST1", 16))
            p2st["stg_free"][tpar] = stq
            out_sem = stq
        p2st["wo_free"][par] = C["sPE"]
        fstate["fb_free"][par] = C["sPE"]
        if oc + 2 < NOC2:
            wo_dmas[oc + 2] = sch.emit("gpsimd", lambda g, oc=oc, par=par: g.dma_start(
                out=wob[par][:, :], in_=wod[oc + 2].rearrange("p dc i -> p (dc i)"), max_dma_last_dim=8192),
                waits=[("sPE", p2st["wo_free"][par])], inc=("sWo0" if par == 0 else "sWo1", 16))
            emit_fchain256(oc + 2)

    sch.emit("sync", lambda s: s.nop(), waits=[("sST0", C["sST0"]), ("sST1", C["sST1"])])
    p2.close()

    with nc.Block() as block:
        @block.sync
        def _(eng):
            sch.replay("sync", eng, sems)

        @block.gpsimd
        def _(eng):
            sch.replay("gpsimd", eng, sems)

        @block.vector
        def _(eng):
            sch.replay("vector", eng, sems)

        @block.scalar
        def _(eng):
            sch.replay("scalar", eng, sems)

        @block.tensor
        def _(eng):
            sch.replay("tensor", eng, sems)

    ctx.close()
    return nc


def _prep_host(inputs):
    g = lambda k: np.asarray(inputs[k], dtype=np.float32)

    def wtile(w, ncol):
        nh = D // ncol
        return np.ascontiguousarray(w.reshape(nh, ncol, NDC, 128).transpose(0, 3, 2, 1))

    def btile(b):
        return np.ascontiguousarray(b.T.reshape(NDC, 128, R).transpose(1, 0, 2).reshape(128, NDC * R))

    m = {}
    m["wq"] = wtile(g("L0_attn_Wq"), 128)
    m["wk"] = wtile(g("L0_attn_Wk"), 128)
    m["wv"] = wtile(g("L0_attn_Wv"), 256)
    m["wo"] = wtile(g("L0_attn_Wo"), 256)
    m["bqc"] = np.ascontiguousarray(g("L0_attn_bq").reshape(H, 128).T)
    m["bkc"] = np.ascontiguousarray(g("L0_attn_bk").reshape(H, 128).T)
    m["bvr"] = np.ascontiguousarray(g("L0_attn_bv").reshape(1, D))
    m["brow"] = np.ascontiguousarray(np.stack([g("L0_attn_bo"), g("L0_ffn_bias")]))
    m["bt"] = np.ascontiguousarray(np.stack([btile(g("L0_attn_Bq")), btile(g("L0_attn_Bk")), btile(g("L0_attn_Bv"))]))
    m["bot"] = btile(g("L0_attn_Bo"))
    m["at"] = np.ascontiguousarray(np.stack([g("L0_attn_Aq").T, g("L0_attn_Ak").T, g("L0_attn_Av").T]))
    m["aot"] = np.ascontiguousarray(g("L0_attn_Ao").T)
    f = g("L0_ffn_factors")
    ft = f.reshape(RF, 4, 4, 128, 8, 256).transpose(0, 4, 1, 3, 2, 5)
    m["fact"] = np.ascontiguousarray(ft.reshape(RF, 8, 4, 128, 1024))
    m["eye"] = np.eye(128, dtype=np.float32)
    return m


def kernel(**inputs):
    nc = build_program()
    shared = _prep_host(inputs)
    x = np.asarray(inputs["L0_attn_x"], dtype=np.float32)
    in_maps = []
    for b in range(B):
        im = dict(shared)
        im["x"] = np.ascontiguousarray(x[b])
        in_maps.append(im)
    res = run_bass_kernel_spmd(nc, in_maps, core_ids=list(range(B)))
    out = np.stack([res.results[b]["out"] for b in range(B)], axis=0)
    return out.astype(np.float32)


if __name__ == "__main__":
    import reference
    inp = reference.setup_inputs()
    out = kernel(**{k: np.asarray(v) for k, v in inp.items()})
    ref = np.asarray(reference.reference(**inp))
    err = float(np.sqrt(np.mean((out - ref) ** 2) / np.mean(ref ** 2)))
    print("rel l2:", err)
